# version 02 (final)
# baseline (speedup 1.0000x reference)
import contextlib
import numpy as np
import ml_dtypes
import concourse.bass as bass
import concourse.mybir as mybir
from concourse.bass_utils import run_bass_kernel_spmd

F32 = mybir.dt.float32
BF16 = mybir.dt.bfloat16
I32 = mybir.dt.int32
AF = mybir.ActivationFunctionType
ALU = mybir.AluOpType
AX = mybir.AxisListType

NCORES = 8
B = 2
S = 4096
D = 4096
TPC = 1024
NQH = 32
NKVH = 8
GQA = NQH // NKVH
HD = 128
H = 11008
HKB = 86
HPKB = 88
HP = HPKB * 128
DCH = D // 128
NTT = 2
SCALE = float(HD) ** -0.5
EPS = 1e-5
BINF = 1.1
KVROWS = 1024 + 1024 + 8
KCROWS = 512 + 512 + 8

bfloat16 = ml_dtypes.bfloat16



class Sems:
    def __init__(self, nc, st):
        self.nc, self.st = nc, st
        self._h, self._n = {}, {}

    def h(self, name):
        if name not in self._h:
            self._h[name] = self.st.enter_context(self.nc.semaphore(name))
            self._n[name] = 0
        return self._h[name]

    def n(self, name):
        self.h(name)
        return self._n[name]

    def bump(self, name, k):
        self.h(name)
        self._n[name] += k


class DStream:

    def __init__(self, sems, name, np_=2, per=1):
        self.s = sems
        self.name = name
        self.np = np_
        self.per = per
        self._issued = {}

    def sem(self, k):
        return self.s.h(f"{self.name}_{k % self.np}")

    def dma(self, q, out, in_, k):
        ins = q.dma_start(out=out, in_=in_)
        ins.then_inc(self.sem(k), 16)
        self.s.bump(f"{self.name}_{k % self.np}", 16)
        self._issued[k] = self._issued.get(k, 0) + 1
        assert self._issued[k] <= self.per, (self.name, k)
        return ins

    def wait(self, eng, k):
        eng.wait_ge(self.sem(k), 16 * self.per * (k // self.np + 1))

class Ctx:
    def __init__(self, nc, st, block, sems):
        self.nc, self.st, self.block, self.sems = nc, st, block, sems

    def sbuf(self, name, shape, dtype):
        return self.st.enter_context(self.nc.sbuf_tensor(name, shape, dtype))

    def psum(self, name, shape, dtype=F32):
        return self.st.enter_context(self.nc.psum_tensor(name, shape, dtype))

    @contextlib.contextmanager
    def scoped(self):
        with contextlib.ExitStack() as inner:
            yield Ctx(self.nc, inner, self.block, self.sems)


def _mk_ctx(nc):
    st = contextlib.ExitStack()
    sems = Sems(nc, st)
    block = st.enter_context(nc.Block())
    return Ctx(nc, st, block, sems), st


def _bass():
    return bass.Bass("TRN2", target_bir_lowering=False, debug=False,
                     detect_race_conditions=False)


def _io(nc, name, shape, dtype, out=False):
    return nc.dram_tensor(name, shape, dtype,
                          kind="ExternalOutput" if out else "ExternalInput").ap()



def ep_wait(s, tag, eng, n):
    if f"{tag}ep0" in s._h or n == 0:
        if n > 0:
            eng.wait_ge(s.h(f"{tag}ep0"), 16 * ((n + 1) // 2))
            if n > 1:
                eng.wait_ge(s.h(f"{tag}ep1"), 16 * (n // 2))
    else:
        eng.wait_ge(s.h(f"{tag}ep"), s_ep_amt(s, tag) * n)


def s_ep_amt(s, tag):
    return 1


def emit_load_xn(ctx, xnT_dram, xn_sb, sem="xnld"):
    s = ctx.sems

    def sec(sync):
        for dc in range(DCH):
            sync.dma_start(out=xn_sb[:, dc * TPC:(dc + 1) * TPC],
                           in_=xnT_dram[dc * 128:(dc + 1) * 128, :]
                           ).then_inc(s.h(sem), 16)
        s.bump(sem, 16 * DCH)

    ctx.block.sync(sec)



def emit_lnT(ctx0, xT, xnT, rows_d):
    with ctx0.scoped() as ctx:
        nc, s, block = ctx.nc, ctx.sems, ctx.block
        stg = [ctx.sbuf(f"ln_stg{i}", [128, 512], F32) for i in range(2)]
        sq = [ctx.sbuf(f"ln_sq{i}", [128, 512], F32) for i in range(2)]
        onesc = ctx.sbuf("ln_ones", [128, 1], F32)
        r_sum = ctx.sbuf("ln_rsum", [1, TPC], F32)
        r_ssq = ctx.sbuf("ln_rssq", [1, TPC], F32)
        r_invr = ctx.sbuf("ln_rinv", [1, TPC], F32)
        r_nmi = ctx.sbuf("ln_rnmi", [1, TPC], F32)
        bci = ctx.sbuf("ln_bci", [128, TPC], F32)
        bcn = ctx.sbuf("ln_bcn", [128, TPC], F32)
        xno = [ctx.sbuf(f"ln_xno{i}", [128, 512], BF16) for i in range(2)]
        p_sum = ctx.psum("ln_psum", [1, 512], F32)
        p_ssq = ctx.psum("ln_pssq", [1, 512], F32)

        NT = NTT * DCH
        ld1 = DStream(s, "ln_ld1")
        ld2 = DStream(s, "ln_ld2")
        sst = DStream(s, "ln_sst")

        def tile(dc, tt):
            return xT[dc * 128:(dc + 1) * 128, tt * 512:(tt + 1) * 512]

        def sec_gp(g):
            g.memset(onesc[:, :], 1.0).then_inc(s.h("ln_cst"), 1)
            s.bump("ln_cst", 1)
            g.wait_ge(s.h("ln_fin"), 1)
            g.dma_start(out=rows_d[0:1, :], in_=r_invr[0:1, :]).then_inc(
                s.h("ln_rd"), 16)
            g.dma_start(out=rows_d[1:2, :], in_=r_nmi[0:1, :]).then_inc(
                s.h("ln_rd"), 16)
            g.wait_ge(s.h("ln_rd"), 32)
            g.dma_start(out=bci[:, :],
                        in_=rows_d[0:1, :].broadcast_to([128, TPC])).then_inc(
                s.h("ln_bc"), 16)
            g.dma_start(out=bcn[:, :],
                        in_=rows_d[1:2, :].broadcast_to([128, TPC])).then_inc(
                s.h("ln_bc"), 16)
            s.bump("ln_rd", 32)
            s.bump("ln_bc", 32)

        def sec_sync(sync):
            for tt in range(NTT):
                for dc in range(DCH):
                    k = tt * DCH + dc
                    if k >= 2:
                        sync.wait_ge(s.h("ln_pe"), 2 * (k - 1))
                    ld1.dma(sync, stg[k % 2][:, :], tile(dc, tt), k)
            sync.wait_ge(s.h("ln_bc"), 32)
            tiles = [(tt, dc) for tt in range(NTT) for dc in range(DCH)]
            for k in range(NT + 1):
                if k < NT:
                    tt, dc = tiles[k]
                    if k >= 2:
                        sync.wait_ge(s.h("ln_ap"), k - 1)
                    ld2.dma(sync, stg[k % 2][:, :], tile(dc, tt), k)
                if k >= 1:
                    tt, dc = tiles[k - 1]
                    sync.wait_ge(s.h("ln_ap"), k)
                    sst.dma(sync, xnT[dc * 128:(dc + 1) * 128,
                                      tt * 512:(tt + 1) * 512],
                            xno[(k - 1) % 2][:, :], k - 1)
            sst.wait(sync, NT - 1)
            sst.wait(sync, NT - 2)

        def sec_act(sc):
            for k in range(NT):
                ld1.wait(sc, k)
                if k >= 2:
                    sc.wait_ge(s.h("ln_pe"), 2 * (k - 1))
                sc.activation(sq[k % 2][:, :], stg[k % 2][:, :],
                              AF.Square).then_inc(s.h("ln_sq"), 1)
            s.bump("ln_sq", NT)
            sc.wait_ge(s.h("ln_var"), 1)
            sc.activation(r_ssq[0:1, :], r_invr[0:1, :], AF.Sqrt).then_inc(
                s.h("ln_sqr"), 1)
            s.bump("ln_sqr", 1)

        def sec_pe(t):
            t.wait_ge(s.h("ln_cst"), 1)
            for tt in range(NTT):
                if tt >= 1:
                    t.wait_ge(s.h("ln_ev"), tt)
                for dc in range(DCH):
                    k = tt * DCH + dc
                    ld1.wait(t, k)
                    t.matmul(p_sum[:, :], onesc[:, :], stg[k % 2][:, :],
                             start=(dc == 0), stop=(dc == DCH - 1))
                    t.wait_ge(s.h("ln_sq"), k + 1)
                    t.matmul(p_ssq[:, :], onesc[:, :], sq[k % 2][:, :],
                             start=(dc == 0), stop=(dc == DCH - 1)).then_inc(
                        s.h("ln_pe"), 2)
            s.bump("ln_pe", 2 * NT)

        def sec_vec(v):
            for tt in range(NTT):
                v.wait_ge(s.h("ln_pe"), 2 * DCH * (tt + 1))
                v.tensor_copy(r_sum[0:1, tt * 512:(tt + 1) * 512], p_sum[:, :])
                v.tensor_copy(r_ssq[0:1, tt * 512:(tt + 1) * 512],
                              p_ssq[:, :]).then_inc(s.h("ln_ev"), 1)
            s.bump("ln_ev", NTT)
            mu = r_sum[0:1, :]
            v.tensor_scalar(mu, mu, 1.0 / D, None, ALU.mult)
            v.tensor_tensor(r_invr[0:1, :], mu, mu, ALU.mult)
            v.scalar_tensor_tensor(r_invr[0:1, :], r_ssq[0:1, :], 1.0 / D,
                                   r_invr[0:1, :], ALU.mult, ALU.subtract)
            v.tensor_scalar(r_invr[0:1, :], r_invr[0:1, :], EPS, None,
                            ALU.add).then_inc(s.h("ln_var"), 1)
            s.bump("ln_var", 1)
            v.wait_ge(s.h("ln_sqr"), 1)
            v.reciprocal(r_invr[0:1, :], r_ssq[0:1, :])
            v.memset(sq[0][:, :], 0.0)
            v.scalar_tensor_tensor(r_nmi[0:1, :], mu, -1.0, r_invr[0:1, :],
                                   ALU.mult, ALU.mult)
            v.memset(sq[1][:, :], 0.0).then_inc(s.h("ln_fin"), 1)
            s.bump("ln_fin", 1)
            for tt in range(NTT):
                for dc in range(DCH):
                    k = tt * DCH + dc
                    ld2.wait(v, k)
                    if k >= 2:
                        sst.wait(v, k - 2)
                    v.tensor_tensor(stg[k % 2][:, :], stg[k % 2][:, :],
                                    bci[:, tt * 512:(tt + 1) * 512], ALU.mult)
                    v.tensor_tensor(xno[k % 2][:, :], stg[k % 2][:, :],
                                    bcn[:, tt * 512:(tt + 1) * 512],
                                    ALU.add).then_inc(s.h("ln_ap"), 1)
            s.bump("ln_ap", NT)

        block.gpsimd(sec_gp)
        block.sync(sec_sync)
        block.tensor(sec_pe)
        block.vector(sec_vec)
        block.scalar(sec_act)


def build_LN(xname, outname):
    nc = _bass()
    xT = _io(nc, xname, [D, TPC], F32)
    xnT = _io(nc, outname, [D, TPC], BF16, out=True)
    rows_d = nc.dram_tensor("ln_rows_d", [2, TPC], F32).ap()
    ctx, st = _mk_ctx(nc)
    with st:
        emit_lnT(ctx, xT, xnT, rows_d)
    return nc



def emit_dq(ctx0, tag, wr, sr, nk, width, out):
    with ctx0.scoped() as ctx:
        s, block = ctx.sems, ctx.block
        HWID = 2752 if width == H else width
        NSPL = width // HWID
        wi = [ctx.sbuf(f"{tag}wi{i}", [64, HWID], I32) for i in range(2)]
        tmpi = ctx.sbuf(f"{tag}tmp", [64, HWID], I32)
        scm = [ctx.sbuf(f"{tag}scm{i}", [64, HWID], F32) for i in range(2)]
        scl = [ctx.sbuf(f"{tag}scl{i}", [64, HWID], F32) for i in range(2)]
        dqt = [ctx.sbuf(f"{tag}dq{i}", [128, HWID], BF16) for i in range(2)]
        NTOT = nk * NSPL
        ld = DStream(s, f"{tag}l", per=3)
        sst = DStream(s, f"{tag}s")

        def sec_sync(sync):
            tiles = [(i, hh) for i in range(nk) for hh in range(NSPL)]
            for k in range(NTOT + 1):
                if k < NTOT:
                    i, hh = tiles[k]
                    if k >= 2:
                        sync.wait_ge(s.h(f"{tag}c"), k - 1)
                    cs = hh * HWID
                    ld.dma(sync, wi[k % 2][:, :], wr[i, :, cs:cs + HWID], k)
                    ld.dma(sync, scm[k % 2][:, :],
                           sr[i, 0:1, cs:cs + HWID].broadcast_to([64, HWID]), k)
                    ld.dma(sync, scl[k % 2][:, :],
                           sr[i, 1:2, cs:cs + HWID].broadcast_to([64, HWID]), k)
                if k >= 1:
                    i, hh = tiles[k - 1]
                    sync.wait_ge(s.h(f"{tag}c"), k)
                    sst.dma(sync, out[i * 128:(i + 1) * 128,
                                      hh * HWID:hh * HWID + HWID],
                            dqt[(k - 1) % 2][:, :], k - 1)
            sst.wait(sync, NTOT - 1)
            if NTOT >= 2:
                sst.wait(sync, NTOT - 2)

        def sec_vec(v):
            for k in range(NTOT):
                ld.wait(v, k)
                if k >= 2:
                    sst.wait(v, k - 2)
                v.tensor_single_scalar(tmpi[:, :], wi[k % 2][:, :], 4,
                                       ALU.arith_shift_right)
                v.tensor_tensor(dqt[k % 2][0:64, :], tmpi[:, :],
                                scm[k % 2][:, :], ALU.mult)
                v.tensor_scalar(tmpi[:, :], wi[k % 2][:, :], 28, 28,
                                ALU.arith_shift_left, ALU.arith_shift_right)
                v.tensor_tensor(dqt[k % 2][64:128, :], tmpi[:, :],
                                scl[k % 2][:, :], ALU.mult).then_inc(
                    s.h(f"{tag}c"), 1)
            s.bump(f"{tag}c", NTOT)

        block.sync(sec_sync)
        block.vector(sec_vec)


def build_DQ():
    nc = _bass()
    w1r = _io(nc, "w1r", [4, 64, H], I32)
    s1r = _io(nc, "s1r", [4, 2, H], F32)
    w3r = _io(nc, "w3r", [4, 64, H], I32)
    s3r = _io(nc, "s3r", [4, 2, H], F32)
    w2r = _io(nc, "w2r", [11, 64, D], I32)
    s2r = _io(nc, "s2r", [11, 2, D], F32)
    dq1 = _io(nc, "dq1", [512, H], BF16, out=True)
    dq3 = _io(nc, "dq3", [512, H], BF16, out=True)
    dq2 = _io(nc, "dq2", [1408, D], BF16, out=True)
    ctx, st = _mk_ctx(nc)
    with st:
        emit_dq(ctx, "q1", w1r, s1r, 4, H, dq1)
        emit_dq(ctx, "q3", w3r, s3r, 4, H, dq3)
        emit_dq(ctx, "q2", w2r, s2r, 11, D, dq2)
    return nc



def emit_proj(ctx0, tag, w_ap, w_f32, dout, xn_sb, epilogue, gsz=2,
              xn_gate=None, gate=None, tail_mm=None):
    with ctx0.scoped() as ctx:
        s, block = ctx.sems, ctx.block
        NDT = dout // 128
        NG = (NDT + gsz - 1) // gsz
        ndts = [min(gsz, NDT - g * gsz) for g in range(NG)]
        cum = [0]
        for g in range(NG):
            cum.append(cum[-1] + ndts[g] * NTT)
        wb = [ctx0.sbuf(f"{tag}wb{i}", [128, gsz * 128], BF16) for i in range(2)]
        pp = [ctx.psum(f"{tag}pp{i}", [128, 512], F32)
              for i in range(2 * gsz * NTT)]
        wld = DStream(s, f"{tag}w")

        _order0 = [k for k in ("act", "vec", "sync", "gp") if k in epilogue]
        _dma0 = epilogue.get("dma_stages", set())
        EPA = 16 if epilogue["final"] in _dma0 else 1
        if EPA == 16:
            s.h(f"{tag}ep0")
            s.h(f"{tag}ep1")

        def ps(gi, j, tt):
            return pp[(gi % 2) * gsz * NTT + j * NTT + tt]

        def sec_wload(q):
            for gi in range(NG):
                ndt = ndts[gi]
                for dc in range(DCH):
                    k = gi * DCH + dc
                    if k >= 2:
                        q.wait_ge(s.h(f"{tag}mm"), k - 1)
                    wld.dma(q, wb[k % 2][:, 0:ndt * 128],
                            w_ap[dc * 128:(dc + 1) * 128,
                                 gi * gsz * 128:(gi * gsz + ndt) * 128], k)

        def sec_pe(t):
            if xn_gate is not None:
                t.wait_ge(s.h(xn_gate), 16 * DCH)
            if gate is not None:
                t.wait_ge(s.h(gate[0]), gate[1])
            for gi in range(NG):
                ndt = ndts[gi]
                for dc in range(DCH):
                    k = gi * DCH + dc
                    wld.wait(t, k)
                    if dc == 0 and gi >= 2:
                        if EPA == 16:
                            ep_wait(s, tag, t, cum[gi - 1])
                        else:
                            t.wait_ge(s.h(f"{tag}ep"), cum[gi - 1])
                    ins = None
                    for j in range(ndt):
                        for tt in range(NTT):
                            ins = t.matmul(
                                ps(gi, j, tt)[:, :],
                                wb[k % 2][:, j * 128:(j + 1) * 128],
                                xn_sb[:, dc * TPC + tt * 512:
                                      dc * TPC + tt * 512 + 512],
                                start=(dc == 0),
                                stop=(dc == DCH - 1 and tail_mm is None))
                    if dc == DCH - 1 and tail_mm is not None:
                        for j in range(ndt):
                            for tt in range(NTT):
                                ins = tail_mm(t, gi * gsz + j, tt,
                                              ps(gi, j, tt))
                    ins.then_inc(s.h(f"{tag}mm"), 1)
            s.bump(f"{tag}mm", NG * DCH)

        (block.gpsimd if w_f32 else block.sync)(sec_wload)
        block.tensor(sec_pe)

        order = [k for k in ("act", "vec", "sync", "gp") if k in epilogue]
        final = epilogue["final"]
        dma_stages = epilogue.get("dma_stages", set())
        amt = {k: (16 if k in dma_stages else 1) for k in order}
        ep_amt = amt[final]

        def mk_sec(key):
            ki = order.index(key)

            def sec(eng):
                idx = 0
                for gi in range(NG):
                    for j in range(ndts[gi]):
                        for tt in range(NTT):
                            eng.wait_ge(s.h(f"{tag}mm"), DCH * (gi + 1))
                            if ki > 0:
                                eng.wait_ge(s.h(f"{tag}x{ki}"),
                                            (idx + 1) * amt[order[ki - 1]])
                            ins = epilogue[key](eng, gi * gsz + j, tt,
                                                ps(gi, j, tt), idx)
                            if ki + 1 < len(order):
                                ins.then_inc(s.h(f"{tag}x{ki + 1}"), amt[key])
                            if key == final:
                                if ep_amt == 16:
                                    ins.then_inc(s.h(f"{tag}ep{idx % 2}"), 16)
                                else:
                                    ins.then_inc(s.h(f"{tag}ep"), 1)
                            idx += 1
                if ki + 1 < len(order):
                    s.bump(f"{tag}x{ki + 1}", cum[-1] * amt[key])
                if key == final:
                    if ep_amt == 16:
                        s.bump(f"{tag}ep0", 16 * ((cum[-1] + 1) // 2))
                        s.bump(f"{tag}ep1", 16 * (cum[-1] // 2))
                    else:
                        s.bump(f"{tag}ep", cum[-1])
            return sec

        for key in order:
            getattr(block, {"act": "scalar", "vec": "vector",
                            "sync": "sync", "gp": "gpsimd"}[key])(mk_sec(key))



def emit_kv0(ctx, xn_sb, wk, bk, wv, bvrow, agkv_in):
    s, block = ctx.sems, ctx.block
    bkc = ctx.sbuf("kv_bk", [128, NKVH], F32)
    ktile = ctx.sbuf("kv_kt", [128, NKVH * TPC], BF16)
    onesb = ctx.sbuf("kv_ones", [128, 1], BF16)

    def sec_pre(sync):
        sync.dma_start(out=bkc[:, :], in_=bk).then_inc(s.h("kv_pre"), 16)
        s.bump("kv_pre", 16)

    def sec_gpre(g):
        g.memset(onesb[:, :], 1.0).then_inc(s.h("kv_pre"), 1)
        s.bump("kv_pre", 1)

    block.sync(sec_pre)
    block.gpsimd(sec_gpre)

    def k_act(eng, dt, tt, psum, idx):
        eng.wait_ge(s.h("kv_pre"), 17)
        return eng.activation(ktile[:, dt * TPC + tt * 512:
                                    dt * TPC + tt * 512 + 512],
                              psum[:, :], AF.Identity, bias=bkc[:, dt:dt + 1])

    def k_sync(eng, dt, tt, psum, idx):
        return eng.dma_start(
            out=agkv_in[dt * 128:(dt + 1) * 128, tt * 512:tt * 512 + 512],
            in_=ktile[:, dt * TPC + tt * 512:dt * TPC + tt * 512 + 512])

    emit_proj(ctx, "kp", wk, True, NKVH * HD, xn_sb,
              {"act": k_act, "sync": k_sync, "final": "sync",
               "dma_stages": {"sync"}},
              xn_gate="xnld")

    with ctx.scoped() as c2:
        ksq = c2.sbuf("kv_ksq", [128, TPC], BF16)
        kvfence = c2.sbuf("kv_fence", [128, 256], F32)
        knrow = [c2.sbuf(f"kv_knr{i}", [1, TPC], F32) for i in range(2)]
        p_kn = c2.psum("kv_pkn", [1, 1024], F32)
        knst = DStream(s, "kv_knst")

        def sec_knsq(sc):
            for dt in range(NKVH):
                sc.wait_ge(s.h("kpx1"), (dt + 1) * NTT)
                if dt >= 1:
                    sc.wait_ge(s.h("kv_kn"), dt)
                sc.activation(ksq[:, :], ktile[:, dt * TPC:(dt + 1) * TPC],
                              AF.Square).then_inc(s.h("kv_sq"), 1)
            s.bump("kv_sq", NKVH)

        def sec_knpe(t):
            for dt in range(NKVH):
                t.wait_ge(s.h("kv_sq"), dt + 1)
                if dt == 0:
                    t.wait_ge(s.h("kpx1"), NKVH * NTT)
                t.wait_ge(s.h("kv_knev"), dt)
                t.matmul(p_kn[:, 0:512], onesb[:, :], ksq[:, 0:512],
                         start=True, stop=True)
                t.matmul(p_kn[:, 512:1024], onesb[:, :], ksq[:, 512:1024],
                         start=True, stop=True).then_inc(s.h("kv_kn"), 1)
            s.bump("kv_kn", NKVH)

        def sec_knvec(v):
            for dt in range(NKVH):
                v.wait_ge(s.h("kv_kn"), dt + 1)
                if dt >= 2:
                    knst.wait(v, dt - 2)
                v.tensor_copy(knrow[dt % 2][0:1, :], p_kn[0:1, :])
                v.memset(kvfence[:, :], 0.0).then_inc(s.h("kv_knev"), 1)
            s.bump("kv_knev", NKVH)

        def sec_knst(g):
            for dt in range(NKVH):
                g.wait_ge(s.h("kv_knev"), dt + 1)
                knst.dma(g, agkv_in[2048 + dt:2048 + dt + 1, :],
                         knrow[dt % 2][0:1, :], dt)

        block.scalar(sec_knsq)
        block.tensor(sec_knpe)
        block.vector(sec_knvec)
        block.gpsimd(sec_knst)

    with ctx.scoped() as c3:
        wvb = [c3.sbuf(f"kv_wv{i}", [128, 512], BF16) for i in range(2)]
        bvb = [c3.sbuf(f"kv_bvb{i}", [128, 512], F32) for i in range(2)]
        p_v = [c3.psum(f"kv_pv{i}", [128, 512], F32) for i in range(8)]
        vout = [c3.sbuf(f"kv_vo{i}", [128, 512], BF16) for i in range(2)]
        wvl = DStream(s, "kv_wv")
        bvl = DStream(s, "kv_bv")
        vst = DStream(s, "kv_vst")

        def sec_vload(g):
            for dvh in range(2):
                bvl.dma(g, bvb[dvh][:, :],
                        bvrow[0:1, dvh * 512:dvh * 512 + 512].broadcast_to(
                            [128, 512]), dvh)
                for dc in range(DCH):
                    k = dvh * DCH + dc
                    if k >= 2:
                        g.wait_ge(s.h("kv_vmm"), k - 1)
                    wvl.dma(g, wvb[k % 2][:, :],
                            wv[dc * 128:(dc + 1) * 128,
                               dvh * 512:dvh * 512 + 512], k)

        def sec_vpe(t):
            for dvh in range(2):
                for dc in range(DCH):
                    k = dvh * DCH + dc
                    wvl.wait(t, k)
                    if dc == 0 and dvh >= 1:
                        t.wait_ge(s.h("kv_vev"), dvh * 8)
                    ins = None
                    for tb in range(8):
                        ins = t.matmul(p_v[tb][:, :],
                                       xn_sb[:, dc * TPC + tb * 128:
                                             dc * TPC + tb * 128 + 128],
                                       wvb[k % 2][:, :],
                                       start=(dc == 0), stop=(dc == DCH - 1))
                    ins.then_inc(s.h("kv_vmm"), 1)
            s.bump("kv_vmm", 2 * DCH)

        def sec_vev(v):
            idx = 0
            for dvh in range(2):
                v.wait_ge(s.h("kv_vmm"), DCH * (dvh + 1))
                bvl.wait(v, dvh)
                for tb in range(8):
                    if idx >= 2:
                        vst.wait(v, idx - 2)
                    v.tensor_tensor(vout[idx % 2][:, :], p_v[tb][:, :],
                                    bvb[dvh][:, :], ALU.add).then_inc(
                        s.h("kv_vev"), 1)
                    idx += 1
            s.bump("kv_vev", 16)

        def sec_vst(sync):
            idx = 0
            for dvh in range(2):
                for tb in range(8):
                    sync.wait_ge(s.h("kv_vev"), idx + 1)
                    vst.dma(sync,
                            agkv_in[1024 + tb * 128:1024 + tb * 128 + 128,
                                    dvh * 512:dvh * 512 + 512],
                            vout[idx % 2][:, :], idx)
                    idx += 1
            vst.wait(sync, 15)
            vst.wait(sync, 14)
            ep_wait(s, "kp", sync, NKVH * NTT)

        def sec_end(g):
            knst.wait(g, NKVH - 1)
            knst.wait(g, NKVH - 2)

        block.gpsimd(sec_vload)
        block.tensor(sec_vpe)
        block.vector(sec_vev)
        block.sync(sec_vst)
        block.gpsimd(sec_end)


def build_A0KV():
    nc = _bass()
    xnT = _io(nc, "xn0T", [D, TPC], BF16)
    wk = _io(nc, "wk", [D, NKVH * HD], F32)
    bk = _io(nc, "bk", [128, NKVH], F32)
    wv = _io(nc, "wv", [D, NKVH * HD], F32)
    bvrow = _io(nc, "bvrow", [1, NKVH * HD], F32)
    agkv_in = _io(nc, "agkv_in", [KVROWS, TPC], BF16, out=True)
    ctx, st = _mk_ctx(nc)
    with st:
        xn_sb = ctx.sbuf("xn_sb", [128, DCH * TPC], BF16)
        emit_load_xn(ctx, xnT, xn_sb)
        emit_kv0(ctx, xn_sb, wk, bk, wv, bvrow, agkv_in)
    return nc



def emit_attn(ctx, lay, xn_sb, wq_ap, bq, qrope, kv_loader, knorm_setup,
              thresh_ap, wo_ap, bo, attnT, scr):
    s, block = ctx.sems, ctx.block
    qoT = ctx.sbuf("at_qoT", [128, NQH * TPC], BF16)
    onesb = ctx.sbuf("at_ones", [128, 1], BF16)
    qmax2 = ctx.sbuf("at_qm2", [1, NQH], F32)
    Bneg = ctx.sbuf("at_Bneg", [128, NQH], F32)

    def sec_c0(g):
        g.memset(onesb[:, :], 1.0).then_inc(s.h("at_c0"), 1)
        s.bump("at_c0", 1)

    block.gpsimd(sec_c0)

    if bq is not None:
        bqc = ctx.sbuf("at_bq", [128, NQH], F32)

        def sec_bq(sync):
            sync.dma_start(out=bqc[:, :], in_=bq).then_inc(s.h("at_bql"), 16)
            s.bump("at_bql", 16)

        block.sync(sec_bq)

    def q_act(eng, dt, tt, psum, idx):
        dst = qoT[:, dt * TPC + tt * 512:dt * TPC + tt * 512 + 512]
        if bq is not None:
            eng.wait_ge(s.h("at_bql"), 16)
            return eng.activation(dst, psum[:, :], AF.Identity,
                                  bias=bqc[:, dt:dt + 1])
        return eng.copy(dst, psum[:, :])

    emit_proj(ctx, f"qp{lay}", wq_ap, True, NQH * HD, xn_sb,
              {"act": q_act, "final": "act"}, xn_gate="xnld")
    qp_ep = f"qp{lay}ep"

    if qrope is not None:
        qcos_ap, qsin_ap = qrope
        if True:
            qcos = ctx.sbuf("at_qcos", [128, TPC], F32)
            qsin = ctx.sbuf("at_qsin", [128, TPC], F32)
            rtmp = ctx.sbuf("at_rtmp", [128, TPC], BF16)
            rqc = ctx.sbuf("at_rqc", [128, TPC], BF16)

            def sec_rl(sync):
                sync.dma_start(out=qcos[:, :], in_=qcos_ap).then_inc(
                    s.h("at_rt"), 16)
                sync.dma_start(out=qsin[:, :], in_=qsin_ap).then_inc(
                    s.h("at_rt"), 16)
                s.bump("at_rt", 32)

            def sec_rope(v):
                v.wait_ge(s.h("at_rt"), 32)
                for h in range(NQH):
                    v.wait_ge(s.h(qp_ep), (h + 1) * NTT)
                    qh = qoT[:, h * TPC:(h + 1) * TPC]
                    v.tensor_tensor(rtmp[0:64, :], qh[64:128, :],
                                    qsin[64:128, :], ALU.mult)
                    v.tensor_tensor(rtmp[64:128, :], qh[0:64, :],
                                    qsin[0:64, :], ALU.mult)
                    v.tensor_tensor(rqc[:, :], qh[:, :], qcos[:, :], ALU.mult)
                    v.tensor_tensor(qh[:, :], rqc[:, :], rtmp[:, :],
                                    ALU.add).then_inc(s.h("at_rp"), 1)
                s.bump("at_rp", NQH)

            block.sync(sec_rl)
            block.vector(sec_rope)

    with ctx.scoped() as c3:
        qsq = ctx.sbuf("at_qsq", [128, TPC], BF16)
        p_qn = c3.psum("at_pqn", [1, 1024], F32)
        kmaxr = ctx.sbuf("at_kmr", [1, 8], F32)
        bprod = ctx.sbuf("at_bpr", [1, NQH], F32)

        def sec_qsq(sc):
            for h in range(NQH):
                if qrope is not None:
                    sc.wait_ge(s.h("at_rp"), h + 1)
                else:
                    sc.wait_ge(s.h(qp_ep), (h + 1) * NTT)
                if h >= 1:
                    sc.wait_ge(s.h("at_qn"), h)
                sc.activation(qsq[:, :], qoT[:, h * TPC:(h + 1) * TPC],
                              AF.Square).then_inc(s.h("at_qs"), 1)
            s.bump("at_qs", NQH)
            sc.wait_ge(s.h("at_bp"), 1)
            sc.activation(bprod[0:1, :], bprod[0:1, :], AF.Sqrt,
                          scale=float((BINF * SCALE) ** 2)).then_inc(
                s.h("at_bs"), 1)
            s.bump("at_bs", 1)

        def sec_qnpe(t):
            t.wait_ge(s.h("at_c0"), 1)
            t.wait_ge(s.h(qp_ep), NQH * NTT)
            for h in range(NQH):
                t.wait_ge(s.h("at_qs"), h + 1)
                if h >= 1:
                    t.wait_ge(s.h("at_qnev"), h)
                t.matmul(p_qn[:, 0:512], onesb[:, :], qsq[:, 0:512],
                         start=True, stop=True)
                t.matmul(p_qn[:, 512:1024], onesb[:, :], qsq[:, 512:1024],
                         start=True, stop=True).then_inc(s.h("at_qn"), 1)
            s.bump("at_qn", NQH)

        def sec_qnv(v):
            for h in range(NQH):
                v.wait_ge(s.h("at_qn"), h + 1)
                v.reduce_max(qmax2[0:1, h:h + 1], p_qn[0:1, :],
                             axis=AX.X).then_inc(s.h("at_qnev"), 1)
            s.bump("at_qnev", NQH)
            v.memset(qsq[:, 0:256], 0.0)
            v.wait_ge(s.h("at_kmr"), 16)
            for kvh in range(NKVH):
                v.tensor_tensor(bprod[0:1, kvh * 4:kvh * 4 + 4],
                                qmax2[0:1, kvh * 4:kvh * 4 + 4],
                                kmaxr[0:1, kvh:kvh + 1].broadcast_to([1, 4]),
                                ALU.mult)
            v.memset(qsq[:, 0:256], 0.0).then_inc(s.h("at_bp"), 1)
            s.bump("at_bp", 1)
            v.wait_ge(s.h("at_bs"), 1)
            v.tensor_scalar(bprod[0:1, :], bprod[0:1, :], -1.0, None,
                            ALU.mult)
            v.memset(qsq[:, 0:256], 0.0).then_inc(s.h("at_bn"), 1)
            s.bump("at_bn", 1)

        def sec_bgp(g):
            g.wait_ge(s.h("at_kmd"), 16)
            g.dma_start(out=kmaxr[0:1, :],
                        in_=scr["kmax"].rearrange("a b -> b a")).then_inc(
                s.h("at_kmr"), 16)
            s.bump("at_kmr", 16)
            g.wait_ge(s.h("at_bn"), 1)
            g.dma_start(out=scr["bneg"], in_=bprod[0:1, :]).then_inc(
                s.h("at_bst"), 16)
            g.wait_ge(s.h("at_bst"), 16)
            g.dma_start(out=Bneg[:, :],
                        in_=scr["bneg"].broadcast_to([128, NQH])).then_inc(
                s.h("at_bnd"), 16)
            s.bump("at_bst", 16)
            s.bump("at_bnd", 16)

        knorm_setup(c3)
        block.scalar(sec_qsq)
        block.tensor(sec_qnpe)
        block.vector(sec_qnv)
        block.gpsimd(sec_bgp)

    with ctx.scoped() as c4:
        ksb = [ctx.sbuf(f"at_ksb{i}", [128, S], BF16) for i in range(2)]
        vsb = [ctx.sbuf(f"at_vsb{i}", [128, S], BF16) for i in range(2)]
        ptb = [ctx.sbuf(f"at_ptb{i}", [128, 1024], BF16) for i in range(2)]
        rrow = ctx.sbuf("at_rrow", [1, 512], F32)
        vfence = ctx.sbuf("at_vfence", [128, 256], F32)
        rb = ctx.sbuf("at_rb", [128, 512], F32)
        psc = [c4.psum(f"at_psc{i}", [128, 1024], F32) for i in range(2)]
        pot = [c4.psum(f"at_pot{i}", [128, 512], F32) for i in range(2)]
        prow = [c4.psum(f"at_prow{i}", [1, 512], F32) for i in range(2)]
        kvld = DStream(s, f"at_kv{lay}", np_=2, per=(8 if lay == 0 else 2))
        causal = thresh_ap is not None
        if causal:
            thsb = ctx.sbuf("at_th", [128, 64], F32)
            diag = ctx.sbuf("at_diag", [128, 512], F32)

            def sec_thl(sync):
                sync.dma_start(out=thsb[:, :], in_=thresh_ap).then_inc(
                    s.h("at_thl"), 16)
                s.bump("at_thl", 16)

            def sec_dgen(g):
                g.iota(diag[:, :], [[1, 512]], base=0, channel_multiplier=-1,
                       allow_small_or_imprecise_dtypes=True).then_inc(
                    s.h("at_dg"), 1)
                s.bump("at_dg", 1)

            block.sync(sec_thl)
            block.gpsimd(sec_dgen)

        NSW = NKVH * GQA * NTT
        NPAIR = 16

        def sweep_list():
            out = []
            for kvh in range(NKVH):
                for qi in range(GQA):
                    for qt in range(NTT):
                        out.append((kvh, NKVH * 0 + kvh * GQA + qi, qt))
            return out

        sweeps = sweep_list()

        def sec_kvload(sync):
            for kvh in range(NKVH):
                if kvh >= 2:
                    sync.wait_ge(s.h("at_oev"), (kvh - 1) * GQA * NTT)
                kv_loader(sync, kvh, ksb[kvh % 2], vsb[kvh % 2], kvld)

        def sec_pe(t):
            t.wait_ge(s.h("at_qnev"), NQH)
            t.wait_ge(s.h(qp_ep), NQH * NTT)
            for i, (kvh, h, qt) in enumerate(sweeps):
                if qt == 0 and h % GQA == 0:
                    kvld.wait(t, kvh)
                kk, vv = ksb[kvh % 2], vsb[kvh % 2]
                q_sl = qoT[:, h * TPC + qt * 512:h * TPC + qt * 512 + 512]

                def vpair(g):
                    t.wait_ge(s.h("at_mk" if causal else "at_ex"),
                              i * NPAIR + g + 1)
                    if g == 0:
                        t.wait_ge(s.h("at_oev"), i - 1) if i >= 1 else None
                        t.wait_ge(s.h("at_rc"), i - 1) if i >= 1 else None
                    for c2_ in range(2):
                        ch = 2 * g + c2_
                        pt_sl = ptb[g % 2][:, c2_ * 512:c2_ * 512 + 512]
                        t.matmul(pot[i % 2][:, :], vv[:, ch * 128:ch * 128 + 128],
                                 pt_sl, start=(ch == 0), stop=(ch == 31))
                        ins = t.matmul(prow[i % 2][:, :], onesb[:, :], pt_sl,
                                       start=(ch == 0), stop=(ch == 31))
                    ins.then_inc(s.h("at_pv"), 1)

                for g in range(NPAIR):
                    if i > 0 or g >= 2:
                        t.wait_ge(s.h("at_ex"), i * NPAIR + g - 1)
                    for c2_ in range(2):
                        ch = 2 * g + c2_
                        ins = t.matmul(psc[g % 2][:, c2_ * 512:c2_ * 512 + 512],
                                       kk[:, ch * 128:ch * 128 + 128], q_sl,
                                       start=True, stop=True)
                    ins.then_inc(s.h("at_sc"), 1)
                    if g >= 1:
                        vpair(g - 1)
                vpair(NPAIR - 1)
            s.bump("at_sc", NSW * NPAIR)
            s.bump("at_pv", NSW * NPAIR)

        def sec_exp(sc):
            sc.wait_ge(s.h("at_bnd"), 16)
            for i, (kvh, h, qt) in enumerate(sweeps):
                for g in range(NPAIR):
                    sc.wait_ge(s.h("at_sc"), i * NPAIR + g + 1)
                    if i * NPAIR + g >= 2:
                        sc.wait_ge(s.h("at_pv"), i * NPAIR + g - 1)
                    sc.activation(ptb[g % 2][:, :], psc[g % 2][:, :], AF.Exp,
                                  bias=Bneg[:, h:h + 1],
                                  scale=float(SCALE)).then_inc(s.h("at_ex"), 1)
            s.bump("at_ex", NSW * NPAIR)

        def sec_vswp(v):
            if causal:
                v.wait_ge(s.h("at_thl"), 16)
                v.wait_ge(s.h("at_dg"), 1)
            for i, (kvh, h, qt) in enumerate(sweeps):
                if causal:
                    for g in range(NPAIR):
                        v.wait_ge(s.h("at_ex"), i * NPAIR + g + 1)
                        for c2_ in range(2):
                            ch = 2 * g + c2_
                            pt_sl = ptb[g % 2][:, c2_ * 512:c2_ * 512 + 512]
                            ins = v.scalar_tensor_tensor(
                                pt_sl, diag[:, :],
                                thsb[:, qt * 32 + ch:qt * 32 + ch + 1],
                                pt_sl, ALU.is_ge, ALU.mult)
                        ins.then_inc(s.h("at_mk"), 1)
                v.wait_ge(s.h("at_pv"), (i + 1) * NPAIR)
                v.reciprocal(rrow[0:1, :], prow[i % 2][0:1, :])
                v.memset(vfence[:, :], 0.0).then_inc(s.h("at_rc"), 1)
                v.wait_ge(s.h("at_rl"), 16 * (i + 1))
                v.scalar_tensor_tensor(
                    qoT[:, h * TPC + qt * 512:h * TPC + qt * 512 + 512],
                    pot[i % 2][:, :], 1.0, rb[:, :], ALU.mult,
                    ALU.mult).then_inc(s.h("at_oev"), 1)
            if causal:
                s.bump("at_mk", NSW * NPAIR)
            s.bump("at_rc", NSW)
            s.bump("at_oev", NSW)

        def sec_norm_g(g):
            for i in range(NSW):
                g.wait_ge(s.h("at_rc"), i + 1)
                g.dma_start(out=scr["rrow"], in_=rrow[0:1, :]).then_inc(
                    s.h("at_rs"), 16)
                g.wait_ge(s.h("at_rs"), 16 * (i + 1))
                if i >= 1:
                    g.wait_ge(s.h("at_oev"), i)
                g.dma_start(out=rb[:, :],
                            in_=scr["rrow"].broadcast_to([128, 512])
                            ).then_inc(s.h("at_rl"), 16)
            s.bump("at_rs", 16 * NSW)
            s.bump("at_rl", 16 * NSW)

        block.sync(sec_kvload)
        block.tensor(sec_pe)
        block.scalar(sec_exp)
        block.vector(sec_vswp)
        block.gpsimd(sec_norm_g)

    with ctx.scoped() as c5:
        aout = [ctx.sbuf(f"at_ao{i}", [128, 512], F32) for i in range(2)]
        if bo is not None:
            boc = ctx.sbuf("at_bo", [128, NQH], F32)

            def sec_bo(sync):
                sync.dma_start(out=boc[:, :], in_=bo).then_inc(
                    s.h("at_bol"), 16)
                s.bump("at_bol", 16)

            block.sync(sec_bo)

        def o_act(eng, dt, tt, psum, idx):
            if idx == 0:
                eng.wait_ge(s.h("at_oev"), NSW)
            if idx >= 2:
                ep_wait(s, f"op{lay}", eng, idx - 1)
            if bo is not None:
                eng.wait_ge(s.h("at_bol"), 16)
                return eng.activation(aout[idx % 2][:, :], psum[:, :],
                                      AF.Identity, bias=boc[:, dt:dt + 1])
            return eng.copy(aout[idx % 2][:, :], psum[:, :])

        def o_sync(eng, dt, tt, psum, idx):
            return eng.dma_start(out=attnT[dt * 128:(dt + 1) * 128,
                                           tt * 512:tt * 512 + 512],
                                 in_=aout[idx % 2][:, :])

        emit_proj(ctx, f"op{lay}", wo_ap, True, D, qoT,
                  {"act": o_act, "sync": o_sync, "final": "sync",
                   "dma_stages": {"sync"}}, gate=("at_oev", NSW))

        def sec_fin(sync):
            ep_wait(s, f"op{lay}", sync, D // 128 * NTT)

        block.sync(sec_fin)


def build_A0():
    nc = _bass()
    xnT = _io(nc, "xn0T", [D, TPC], BF16)
    agkv = _io(nc, "agkv", [4 * KVROWS, TPC], BF16)
    wq = _io(nc, "wq", [D, NQH * HD], F32)
    bq = _io(nc, "bq", [128, NQH], F32)
    wo = _io(nc, "wo", [D, D], F32)
    bo = _io(nc, "bo", [128, NQH], F32)
    attnT = _io(nc, "attn0T", [D, TPC], F32, out=True)
    scr = {
        "rrow": nc.dram_tensor("scr_rrow", [1, 512], F32).ap(),
        "kmax": nc.dram_tensor("scr_kmax", [8, 1], F32).ap(),
        "bneg": nc.dram_tensor("scr_bneg", [1, NQH], F32).ap(),
    }
    ctx, st = _mk_ctx(nc)
    with st:
        xn_sb = ctx.sbuf("xn_sb", [128, DCH * TPC], BF16)
        emit_load_xn(ctx, xnT, xn_sb)
        s = ctx.sems

        def kv_loader(sync, kvh, kdst, vdst, kvld):
            for r in range(4):
                kvld.dma(sync, kdst[:, r * TPC:(r + 1) * TPC],
                         agkv[r * KVROWS + kvh * 128:
                              r * KVROWS + kvh * 128 + 128, :], kvh)
                vsrc = agkv[r * KVROWS + 1024:r * KVROWS + 2048,
                            kvh * 128:(kvh + 1) * 128]
                kvld.dma(sync,
                         vdst[:, r * TPC:(r + 1) * TPC].rearrange(
                             "p (kb dv) -> p kb dv", kb=8),
                         vsrc.rearrange("(kb p) dv -> p kb dv", p=128), kvh)

        def knorm_setup(c3):
            knall = ctx.sbuf("at_knall", [8, S], BF16)
            kmax2 = ctx.sbuf("at_km2", [8, 1], F32)

            def sec_knl(sync):
                for r in range(4):
                    sync.dma_start(out=knall[:, r * TPC:(r + 1) * TPC],
                                   in_=agkv[r * KVROWS + 2048:
                                            r * KVROWS + 2056, :]).then_inc(
                        s.h("at_knl"), 16)
                s.bump("at_knl", 64)

            def sec_knv(v):
                v.wait_ge(s.h("at_knl"), 64)
                v.reduce_max(kmax2[:, :], knall[:, :], axis=AX.X).then_inc(
                    s.h("at_km2"), 1)
                s.bump("at_km2", 1)

            def sec_kng(g):
                g.wait_ge(s.h("at_km2"), 1)
                g.dma_start(out=scr["kmax"], in_=kmax2[:, :]).then_inc(
                    s.h("at_kmd"), 16)
                s.bump("at_kmd", 16)

            ctx.block.sync(sec_knl)
            ctx.block.vector(sec_knv)
            ctx.block.gpsimd(sec_kng)

        emit_attn(ctx, 0, xn_sb, wq, bq, None, kv_loader, knorm_setup,
                  None, wo, bo, attnT, scr)
    return nc



def emit_f0(ctx, xn_sb, w1t, w3t, w2t, attnT, x0T, x1T):
    s, block = ctx.sems, ctx.block
    hT = ctx.sbuf("f0_hT", [128, HKB * 512], BF16)
    asil = [ctx.sbuf(f"f0_as{i}", [128, 512], BF16) for i in range(2)]
    wub = [ctx.sbuf(f"f0_wu{i}", [128, 2 * 256], BF16) for i in range(2)]
    wdb = [ctx.sbuf(f"f0_wd{i}", [128, 256], BF16) for i in range(2)]
    at_b = [ctx.sbuf(f"f0_at{i}", [128, 512], F32) for i in range(2)]
    x0_b = [ctx.sbuf(f"f0_x0{i}", [128, 512], F32) for i in range(2)]
    x1o = [ctx.sbuf(f"f0_x1{i}", [128, 512], F32) for i in range(2)]

    GU = 2
    NGU = (HKB + GU - 1) // GU
    gu_sizes = [min(GU, HKB - g * GU) for g in range(NGU)]
    GD = 2
    NGD = (DCH + GD - 1) // GD
    gd_sizes = [min(GD, DCH - g * GD) for g in range(NGD)]

    wul = DStream(s, "f0_wul", per=2)
    wdl = DStream(s, "f0_wdl")
    rsl = DStream(s, "f0_rsl", per=2)
    xst = DStream(s, "f0_xst")

    pa = [ctx.psum(f"f0_pa{i}", [128, 512], F32) for i in range(4)]
    pb = [ctx.psum(f"f0_pb{i}", [128, 512], F32) for i in range(4)]

    for tt in range(NTT):
        base_h = s.n("f0h")
        base_sil = s.n("f0sil")
        base_ups = s.n("f0ups")
        base_dn = s.n("f0dn")
        base_dnp = s.n("f0dnp")

        def sec_wu(sync, tt=tt):
            for g in range(NGU):
                ng = gu_sizes[g]
                for dc in range(DCH):
                    k = tt * NGU * DCH + g * DCH + dc
                    if k >= 2:
                        sync.wait_ge(s.h("f0_umm"), k - 1)
                    h0 = g * GU * 128
                    wul.dma(sync, wub[k % 2][:, 0:ng * 128],
                            w1t[dc * 128:(dc + 1) * 128, h0:h0 + ng * 128], k)
                    wul.dma(sync, wub[k % 2][:, 256:256 + ng * 128],
                            w3t[dc * 128:(dc + 1) * 128, h0:h0 + ng * 128], k)

        def sec_upe(t, tt=tt, base_h=base_h):
            if tt >= 1:
                t.wait_ge(s.h("f0dn"), 32)
            for g in range(NGU):
                ng = gu_sizes[g]
                if g >= 2:
                    t.wait_ge(s.h("f0h"), base_h + (g - 1) * GU)
                st_ = (g % 2) * 2
                for dc in range(DCH):
                    k = tt * NGU * DCH + g * DCH + dc
                    wul.wait(t, k)
                    ins2 = None
                    for j in range(ng):
                        t.matmul(pa[st_ + j][:, :],
                                 wub[k % 2][:, j * 128:j * 128 + 128],
                                 xn_sb[:, dc * TPC + tt * 512:
                                       dc * TPC + tt * 512 + 512],
                                 start=(dc == 0), stop=(dc == DCH - 1))
                        ins2 = t.matmul(
                            pb[st_ + j][:, :],
                            wub[k % 2][:, 256 + j * 128:256 + j * 128 + 128],
                            xn_sb[:, dc * TPC + tt * 512:
                                  dc * TPC + tt * 512 + 512],
                            start=(dc == 0), stop=(dc == DCH - 1))
                    ins2.then_inc(s.h("f0_umm"), 1)
            s.bump("f0_umm", NGU * DCH)

        def sec_usil(sc, tt=tt, base_h=base_h, base_sil=base_sil,
                     base_ups=base_ups):
            idx = 0
            for g in range(NGU):
                ng = gu_sizes[g]
                sc.wait_ge(s.h("f0_umm"), tt * NGU * DCH + (g + 1) * DCH)
                st_ = (g % 2) * 2
                for j in range(ng):
                    if idx >= 2:
                        sc.wait_ge(s.h("f0h"), base_h + idx - 1)
                    sc.activation(asil[idx % 2][:, :], pa[st_ + j][:, :],
                                  AF.Silu).then_inc(s.h("f0sil"), 1)
                    idx += 1
            s.bump("f0sil", HKB)

        def sec_uh(v, tt=tt, base_sil=base_sil):
            idx = 0
            for g in range(NGU):
                ng = gu_sizes[g]
                st_ = (g % 2) * 2
                for j in range(ng):
                    ht = g * GU + j
                    v.wait_ge(s.h("f0sil"), base_sil + idx + 1)
                    v.scalar_tensor_tensor(
                        hT[:, ht * 512:ht * 512 + 512], pb[st_ + j][:, :], 1.0,
                        asil[idx % 2][:, :], ALU.mult, ALU.mult).then_inc(
                        s.h("f0h"), 1)
                    idx += 1
            s.bump("f0h", HKB)

        block.sync(sec_wu)
        block.tensor(sec_upe)
        block.scalar(sec_usil)
        block.vector(sec_uh)

        def sec_wd(sync, tt=tt):
            for g in range(NGD):
                ng = gd_sizes[g]
                for hc in range(HKB):
                    k = tt * NGD * HKB + g * HKB + hc
                    if k >= 2:
                        sync.wait_ge(s.h("f0_dmm"), k - 1)
                    d0 = g * GD * 128
                    wdl.dma(sync, wdb[k % 2][:, 0:ng * 128],
                            w2t[hc * 128:(hc + 1) * 128, d0:d0 + ng * 128], k)

        def sec_dres(g_, tt=tt):
            for dt in range(DCH + 1):
                if dt < DCH:
                    k = tt * DCH + dt
                    if k >= 2:
                        g_.wait_ge(s.h("f0dn"), (k - 1))
                    rsl.dma(g_, at_b[k % 2][:, :],
                            attnT[dt * 128:(dt + 1) * 128,
                                  tt * 512:tt * 512 + 512], k)
                    rsl.dma(g_, x0_b[k % 2][:, :],
                            x0T[dt * 128:(dt + 1) * 128,
                                tt * 512:tt * 512 + 512], k)
                if dt >= 1:
                    k = tt * DCH + dt - 1
                    g_.wait_ge(s.h("f0dn"), k + 1)
                    xst.dma(g_, x1T[(dt - 1) * 128:dt * 128,
                                    tt * 512:tt * 512 + 512],
                            x1o[k % 2][:, :], k)

        def sec_dpe(t, tt=tt, base_h=base_h, base_dn=base_dn):
            t.wait_ge(s.h("f0h"), base_h + HKB)
            for g in range(NGD):
                ng = gd_sizes[g]
                if g >= 2:
                    t.wait_ge(s.h("f0dn"), base_dn + (g - 1) * GD)
                st_ = (g % 2) * 2
                for hc in range(HKB):
                    k = tt * NGD * HKB + g * HKB + hc
                    wdl.wait(t, k)
                    ins = None
                    for j in range(ng):
                        ins = t.matmul(
                            pb[st_ + j][:, :],
                            wdb[k % 2][:, j * 128:j * 128 + 128],
                            hT[:, hc * 512:hc * 512 + 512],
                            start=(hc == 0), stop=(hc == HKB - 1))
                    ins.then_inc(s.h("f0_dmm"), 1)
            s.bump("f0_dmm", NGD * HKB)

        def sec_dev(v, tt=tt, base_dnp=base_dnp):
            for g in range(NGD):
                ng = gd_sizes[g]
                v.wait_ge(s.h("f0_dmm"), tt * NGD * HKB + (g + 1) * HKB)
                st_ = (g % 2) * 2
                for j in range(ng):
                    dt = g * GD + j
                    k = tt * DCH + dt
                    rsl.wait(v, k)
                    if k >= 2:
                        xst.wait(v, k - 2)
                    v.tensor_add(at_b[k % 2][:, :], at_b[k % 2][:, :],
                                 x0_b[k % 2][:, :])
                    v.scalar_tensor_tensor(
                        x1o[k % 2][:, :], pb[st_ + j][:, :], 1.0,
                        at_b[k % 2][:, :], ALU.mult, ALU.add).then_inc(
                        s.h("f0dn"), 1)
            s.bump("f0dn", DCH)

        block.sync(sec_wd)
        block.gpsimd(sec_dres)
        block.tensor(sec_dpe)
        block.vector(sec_dev)

    def sec_fin(g):
        xst.wait(g, 2 * DCH - 1)
        xst.wait(g, 2 * DCH - 2)

    block.gpsimd(sec_fin)


def build_F0():
    nc = _bass()
    xnT = _io(nc, "xn0T", [D, TPC], BF16)
    w1t = _io(nc, "w1t", [D, H], BF16)
    w3t = _io(nc, "w3t", [D, H], BF16)
    w2t = _io(nc, "w2t", [HP, D], BF16)
    attnT = _io(nc, "attn0T", [D, TPC], F32)
    x0T = _io(nc, "x0T", [D, TPC], F32)
    x1T = _io(nc, "x1T", [D, TPC], F32, out=True)
    ctx, st = _mk_ctx(nc)
    with st:
        xn_sb = ctx.sbuf("xn_sb", [128, DCH * TPC], BF16)
        emit_load_xn(ctx, xnT, xn_sb)

        def gate(t):
            t.wait_ge(ctx.sems.h("xnld"), 16 * DCH)

        ctx.block.tensor(gate)
        emit_f0(ctx, xn_sb, w1t, w3t, w2t, attnT, x0T, x1T)
    return nc



def emit_a1kv(ctx, xn_sb, wkrT, wdkT, kcos_ap, ksin_ap, agkc_in):
    s, block = ctx.sems, ctx.block
    krf = ctx.sbuf("ak_krf", [128, 4 * TPC], F32)
    krb = ctx.sbuf("ak_krb", [128, 4 * TPC], BF16)
    kcos = ctx.sbuf("ak_kcos", [128, TPC], F32)
    ksin = ctx.sbuf("ak_ksin", [128, TPC], F32)
    rtmp = ctx.sbuf("ak_rtmp", [128, TPC], F32)
    rqc = ctx.sbuf("ak_rqc", [128, TPC], F32)
    hob = ctx.sbuf("ak_ho", [128, 2], BF16)
    ksq = ctx.sbuf("ak_ksq", [128, TPC], BF16)
    kn2 = [ctx.sbuf(f"ak_kn2_{i}", [2, TPC], F32) for i in range(2)]
    ckb = [ctx.sbuf(f"ak_ck{i}", [128, 512], BF16) for i in range(2)]
    krst = DStream(s, "ak_krst")
    knst = DStream(s, "ak_knst")

    def sec_c(g):
        g.memset(hob[:, :], 0.0)
        g.memset(hob[0:64, 0:1], 1.0)
        g.memset(hob[64:128, 1:2], 1.0).then_inc(s.h("ak_c"), 1)
        s.bump("ak_c", 1)

    def sec_tl(sync):
        sync.dma_start(out=kcos[:, :], in_=kcos_ap).then_inc(s.h("ak_tl"), 16)
        sync.dma_start(out=ksin[:, :], in_=ksin_ap).then_inc(s.h("ak_tl"), 16)
        s.bump("ak_tl", 32)

    block.gpsimd(sec_c)
    block.sync(sec_tl)

    def kr_act(eng, dt, tt, psum, idx):
        return eng.copy(krf[:, dt * TPC + tt * 512:dt * TPC + tt * 512 + 512],
                        psum[:, :])

    emit_proj(ctx, "akr", wkrT, True, 512, xn_sb,
              {"act": kr_act, "final": "act"}, xn_gate="xnld")

    def sec_rope(v):
        v.wait_ge(s.h("ak_tl"), 32)
        for dt in range(4):
            v.wait_ge(s.h("akrep"), (dt + 1) * NTT)
            kf = krf[:, dt * TPC:(dt + 1) * TPC]
            kb = krb[:, dt * TPC:(dt + 1) * TPC]
            v.tensor_tensor(rtmp[0:32, :], kf[32:64, :], ksin[32:64, :],
                            ALU.mult)
            v.tensor_tensor(rtmp[32:64, :], kf[0:32, :], ksin[0:32, :],
                            ALU.mult)
            v.tensor_tensor(rtmp[64:96, :], kf[96:128, :], ksin[96:128, :],
                            ALU.mult)
            v.tensor_tensor(rtmp[96:128, :], kf[64:96, :], ksin[64:96, :],
                            ALU.mult)
            v.tensor_tensor(rqc[:, :], kf[:, :], kcos[:, :], ALU.mult)
            v.tensor_tensor(kb[:, :], rqc[:, :], rtmp[:, :],
                            ALU.add).then_inc(s.h("ak_rp"), 1)
        s.bump("ak_rp", 4)

    def sec_krst(g):
        for dt in range(4):
            g.wait_ge(s.h("ak_rp"), dt + 1)
            krst.dma(g, agkc_in[dt * 128:(dt + 1) * 128, :],
                     krb[:, dt * TPC:(dt + 1) * TPC], dt)

    def sec_ksq(sc):
        for dt in range(4):
            sc.wait_ge(s.h("ak_rp"), dt + 1)
            if dt >= 1:
                sc.wait_ge(s.h("ak_kn"), dt)
            sc.activation(ksq[:, :], krb[:, dt * TPC:(dt + 1) * TPC],
                          AF.Square).then_inc(s.h("ak_sq"), 1)
        s.bump("ak_sq", 4)

    with ctx.scoped() as c2:
        p_kn = c2.psum("ak_pkn", [2, 1024], F32)

        def sec_knpe(t):
            t.wait_ge(s.h("ak_c"), 1)
            t.wait_ge(s.h("akrep"), 4 * NTT)
            for dt in range(4):
                t.wait_ge(s.h("ak_sq"), dt + 1)
                if dt >= 1:
                    t.wait_ge(s.h("ak_knev"), dt)
                t.matmul(p_kn[:, 0:512], hob[:, :], ksq[:, 0:512],
                         start=True, stop=True)
                t.matmul(p_kn[:, 512:1024], hob[:, :], ksq[:, 512:1024],
                         start=True, stop=True).then_inc(s.h("ak_kn"), 1)
            s.bump("ak_kn", 4)

        def sec_knv(v):
            for dt in range(4):
                v.wait_ge(s.h("ak_kn"), dt + 1)
                if dt >= 2:
                    knst.wait(v, dt - 2)
                v.tensor_copy(kn2[dt % 2][:, :], p_kn[:, :]).then_inc(
                    s.h("ak_knev"), 1)
            s.bump("ak_knev", 4)

        def sec_knst(g):
            for dt in range(4):
                g.wait_ge(s.h("ak_knev"), dt + 1)
                knst.dma(g, agkc_in[1024 + 2 * dt:1024 + 2 * dt + 2, :],
                         kn2[dt % 2][:, :], dt)

        block.vector(sec_rope)
        block.gpsimd(sec_krst)
        block.scalar(sec_ksq)
        block.tensor(sec_knpe)
        block.vector(sec_knv)
        block.gpsimd(sec_knst)

    def ck_act(eng, dt, tt, psum, idx):
        if idx == 0:
            eng.wait_ge(s.h("ak_knev"), 4)
        if idx >= 2:
            ep_wait(s, "adk", eng, idx - 1)
        return eng.copy(ckb[idx % 2][:, :], psum[:, :])

    def ck_sync(eng, dt, tt, psum, idx):
        return eng.dma_start(
            out=agkc_in[512 + dt * 128:512 + dt * 128 + 128,
                        tt * 512:tt * 512 + 512],
            in_=ckb[idx % 2][:, :])

    emit_proj(ctx, "adk", wdkT, True, 512, xn_sb,
              {"act": ck_act, "sync": ck_sync, "final": "sync",
               "dma_stages": {"sync"}}, gate=("ak_knev", 4))

    def sec_fin(sync):
        ep_wait(s, "adk", sync, 8)
        krst.wait(sync, 3)
        krst.wait(sync, 2)

    def sec_fin2(g):
        knst.wait(g, 3)
        knst.wait(g, 2)

    block.sync(sec_fin)
    block.gpsimd(sec_fin2)


def build_A1KV():
    nc = _bass()
    xnT = _io(nc, "xn1T", [D, TPC], BF16)
    wkrT = _io(nc, "wkrT", [D, 512], F32)
    wdkT = _io(nc, "wdkT", [D, 512], F32)
    kcos = _io(nc, "kcos", [128, TPC], F32)
    ksin = _io(nc, "ksinadj", [128, TPC], F32)
    agkc_in = _io(nc, "agkc_in", [KCROWS, TPC], BF16, out=True)
    ctx, st = _mk_ctx(nc)
    with st:
        xn_sb = ctx.sbuf("xn_sb", [128, DCH * TPC], BF16)
        emit_load_xn(ctx, xnT, xn_sb)
        emit_a1kv(ctx, xn_sb, wkrT, wdkT, kcos, ksin, agkc_in)
    return nc



def emit_a1pre(ctx, agkc, wupkT, wupvT, kfull, vfull, kmax_d):
    s, block = ctx.sems, ctx.block
    ckv = [ctx.sbuf(f"ap_ckv{i}", [128, S], BF16) for i in range(4)]
    wupkb = [ctx.sbuf(f"ap_wk{i}", [128, 512], BF16) for i in range(4)]
    wupvb = [ctx.sbuf(f"ap_wv{i}", [128, 1024], BF16) for i in range(4)]
    kcb = [ctx.sbuf(f"ap_kc{i}", [128, 512], BF16) for i in range(2)]
    vob = [ctx.sbuf(f"ap_vo{i}", [128, 512], BF16) for i in range(2)]
    hob = ctx.sbuf("ap_ho", [128, 2], BF16)
    knall2 = ctx.sbuf("ap_kn2", [2, 4 * S], F32)
    knr = ctx.sbuf("ap_knr", [8, S], BF16)
    kcm = ctx.sbuf("ap_kcm", [2, 4], F32)
    krm = ctx.sbuf("ap_krm", [8, 1], F32)
    km2 = ctx.sbuf("ap_km2", [8, 1], F32)
    ksq = ctx.sbuf("ap_ksq", [128, 512], BF16)
    apfence = ctx.sbuf("ap_fence", [128, 256], F32)
    kmtmp = ctx.nc.dram_tensor("ap_kmtmp", [8], F32).ap()

    kcst = DStream(s, "ap_kcst", per=4)
    vst = DStream(s, "ap_vst")

    def sec_c(g):
        g.memset(hob[:, :], 0.0)
        g.memset(hob[0:64, 0:1], 1.0)
        g.memset(hob[64:128, 1:2], 1.0).then_inc(s.h("ap_c"), 1)
        s.bump("ap_c", 1)
        for r in range(4):
            for h in range(NKVH):
                g.dma_start(out=kfull[h * 128:h * 128 + 32,
                                      r * TPC:(r + 1) * TPC],
                            in_=agkc[r * KCROWS + h * 64:
                                     r * KCROWS + h * 64 + 32, :]).then_inc(
                    s.h("ap_krp"), 16)
                g.dma_start(out=kfull[h * 128 + 64:h * 128 + 96,
                                      r * TPC:(r + 1) * TPC],
                            in_=agkc[r * KCROWS + h * 64 + 32:
                                     r * KCROWS + h * 64 + 64, :]).then_inc(
                    s.h("ap_krp"), 16)
        s.bump("ap_krp", 16 * 64)

    def sec_l(sync):
        for cc in range(4):
            for r in range(4):
                sync.dma_start(out=ckv[cc][:, r * TPC:(r + 1) * TPC],
                               in_=agkc[r * KCROWS + 512 + cc * 128:
                                        r * KCROWS + 512 + cc * 128 + 128, :]
                               ).then_inc(s.h("ap_ckl"), 16)
        s.bump("ap_ckl", 16 * 16)
        for r in range(4):
            sync.dma_start(out=knr[:, r * TPC:(r + 1) * TPC],
                           in_=agkc[r * KCROWS + 1024:r * KCROWS + 1032, :]
                           ).then_inc(s.h("ap_knl"), 16)
        s.bump("ap_knl", 64)

    def sec_wl(g):
        for cc in range(4):
            g.dma_start(out=wupkb[cc][:, :],
                        in_=wupkT[cc * 128:(cc + 1) * 128, :]).then_inc(
                s.h("ap_wkl"), 16)
            g.dma_start(out=wupvb[cc][:, :],
                        in_=wupvT[cc * 128:(cc + 1) * 128, :]).then_inc(
                s.h("ap_wvl"), 16)
        s.bump("ap_wkl", 64)
        s.bump("ap_wvl", 64)

    block.gpsimd(sec_c)
    block.sync(sec_l)
    block.gpsimd(sec_wl)

    with ctx.scoped() as c2:
        p_kc = [c2.psum(f"ap_pkc{i}", [128, 512], F32) for i in range(2)]
        p_kn = [c2.psum(f"ap_pkn{i}", [2, 512], F32) for i in range(2)]
        p_v = [c2.psum(f"ap_pv{i}", [128, 512], F32) for i in range(2)]

        def sec_kpe(t):
            t.wait_ge(s.h("ap_ckl"), 256)
            t.wait_ge(s.h("ap_wkl"), 64)
            t.wait_ge(s.h("ap_c"), 1)
            idx = 0
            for kt in range(4):
                for ks in range(8):
                    if idx >= 2:
                        t.wait_ge(s.h("ap_kcc"), idx - 1)
                        t.wait_ge(s.h("ap_knev"), idx - 1)
                    ins = None
                    for cc in range(4):
                        ins = t.matmul(p_kc[idx % 2][:, :],
                                       wupkb[cc][:, kt * 128:(kt + 1) * 128],
                                       ckv[cc][:, ks * 512:ks * 512 + 512],
                                       start=(cc == 0), stop=(cc == 3))
                    ins.then_inc(s.h("ap_kc"), 1)
                    t.wait_ge(s.h("ap_sq"), idx + 1)
                    t.matmul(p_kn[idx % 2][:, :], hob[:, :], ksq[:, :],
                             start=True, stop=True).then_inc(s.h("ap_kn"), 1)
                    idx += 1
            s.bump("ap_kc", 32)
            s.bump("ap_kn", 32)

        def sec_kev(sc):
            idx = 0
            for kt in range(4):
                for ks in range(8):
                    sc.wait_ge(s.h("ap_kc"), idx + 1)
                    if idx >= 2:
                        kcst.wait(sc, idx - 2)
                    sc.copy(kcb[idx % 2][:, :], p_kc[idx % 2][:, :]).then_inc(
                        s.h("ap_kcc"), 1)
                    sc.activation(ksq[:, :], kcb[idx % 2][:, :],
                                  AF.Square).then_inc(s.h("ap_sq"), 1)
                    idx += 1
            s.bump("ap_kcc", 32)
            s.bump("ap_sq", 32)

        def sec_knev(v):
            idx = 0
            for kt in range(4):
                for ks in range(8):
                    v.wait_ge(s.h("ap_kn"), idx + 1)
                    v.tensor_copy(knall2[:, kt * S + ks * 512:
                                         kt * S + ks * 512 + 512],
                                  p_kn[idx % 2][:, :]).then_inc(
                        s.h("ap_knev"), 1)
                    idx += 1
            s.bump("ap_knev", 32)

        def sec_kst(sync):
            idx = 0
            for kt in range(4):
                for ks in range(8):
                    sync.wait_ge(s.h("ap_kcc"), idx + 1)
                    for seg in range(4):
                        hh = 2 * kt + seg // 2
                        base = 32 if (seg % 2 == 0) else 96
                        kcst.dma(sync,
                                 kfull[hh * 128 + base:hh * 128 + base + 32,
                                       ks * 512:ks * 512 + 512],
                                 kcb[idx % 2][seg * 32:seg * 32 + 32, :], idx)
                    idx += 1

        block.tensor(sec_kpe)
        block.scalar(sec_kev)
        block.vector(sec_knev)
        block.sync(sec_kst)

        def sec_kmv(v):
            v.wait_ge(s.h("ap_knev"), 32)
            for kt in range(4):
                v.reduce_max(kcm[:, kt:kt + 1],
                             knall2[:, kt * S:(kt + 1) * S], axis=AX.X)
            v.memset(apfence[:, :], 0.0).then_inc(s.h("ap_kcm"), 4)
            s.bump("ap_kcm", 4)
            v.wait_ge(s.h("ap_knl"), 64)
            v.reduce_max(krm[:, :], knr[:, :], axis=AX.X)
            v.memset(apfence[:, :], 0.0).then_inc(s.h("ap_krm"), 1)
            s.bump("ap_krm", 1)
            v.wait_ge(s.h("ap_kmb"), 16)
            v.tensor_add(km2[:, :], km2[:, :], krm[:, :])
            v.memset(apfence[:, :], 0.0).then_inc(s.h("ap_km2"), 1)
            s.bump("ap_km2", 1)

        def sec_kmg(g):
            g.wait_ge(s.h("ap_kcm"), 4)
            g.dma_start(out=kmtmp.rearrange("(kt two) -> two kt", two=2),
                        in_=kcm[:, :],
                        allow_slow_non_contiguous=True).then_inc(
                s.h("ap_kms"), 16)
            g.wait_ge(s.h("ap_kms"), 16)
            g.dma_start(out=km2[:, :],
                        in_=kmtmp.rearrange("(a b) -> a b", b=1)).then_inc(
                s.h("ap_kmb"), 16)
            s.bump("ap_kms", 16)
            s.bump("ap_kmb", 16)
            g.wait_ge(s.h("ap_km2"), 1)
            g.dma_start(out=kmax_d, in_=km2[:, :]).then_inc(s.h("ap_kmd"), 16)
            s.bump("ap_kmd", 16)

        block.vector(sec_kmv)
        block.gpsimd(sec_kmg)

        def sec_vpe(t):
            t.wait_ge(s.h("ap_wvl"), 64)
            idx = 0
            for kb in range(32):
                for dvh in range(2):
                    if idx >= 2:
                        t.wait_ge(s.h("ap_vev"), idx - 1)
                    ins = None
                    for cc in range(4):
                        ins = t.matmul(
                            p_v[idx % 2][:, :],
                            ckv[cc][:, kb * 128:kb * 128 + 128],
                            wupvb[cc][:, dvh * 512:dvh * 512 + 512],
                            start=(cc == 0), stop=(cc == 3))
                    ins.then_inc(s.h("ap_v"), 1)
                    idx += 1
            s.bump("ap_v", 64)

        def sec_vev(v):
            for idx in range(64):
                v.wait_ge(s.h("ap_v"), idx + 1)
                if idx >= 2:
                    vst.wait(v, idx - 2)
                v.tensor_copy(vob[idx % 2][:, :], p_v[idx % 2][:, :]
                              ).then_inc(s.h("ap_vev"), 1)
            s.bump("ap_vev", 64)

        def sec_vst(g):
            for idx in range(64):
                kb, dvh = idx // 2, idx % 2
                g.wait_ge(s.h("ap_vev"), idx + 1)
                vst.dma(g, vfull[kb * 128:(kb + 1) * 128,
                                 dvh * 512:dvh * 512 + 512],
                        vob[idx % 2][:, :], idx)

        block.tensor(sec_vpe)
        block.vector(sec_vev)
        block.gpsimd(sec_vst)

    def sec_fin(sync):
        kcst.wait(sync, 31)
        kcst.wait(sync, 30)

    def sec_fin2(g):
        vst.wait(g, 63)
        vst.wait(g, 62)
        g.wait_ge(s.h("ap_krp"), 16 * 64)
        g.wait_ge(s.h("ap_kmd"), 16)

    block.sync(sec_fin)
    block.gpsimd(sec_fin2)


def build_A1PRE():
    nc = _bass()
    agkc = _io(nc, "agkc", [4 * KCROWS, TPC], BF16)
    wupkT = _io(nc, "wupkT", [512, 512], F32)
    wupvT = _io(nc, "wupvT", [512, 1024], F32)
    kfull = _io(nc, "kfull", [NKVH * HD, S], BF16, out=True)
    vfull = _io(nc, "vfull", [S, NKVH * HD], BF16, out=True)
    kmax_d = _io(nc, "kmax_d", [8, 1], F32, out=True)
    ctx, st = _mk_ctx(nc)
    with st:
        emit_a1pre(ctx, agkc, wupkT, wupvT, kfull, vfull, kmax_d)
    return nc


def build_A1():
    nc = _bass()
    xnT = _io(nc, "xn1T", [D, TPC], BF16)
    kfull = _io(nc, "kfull", [NKVH * HD, S], BF16)
    vfull = _io(nc, "vfull", [S, NKVH * HD], BF16)
    kmax_in = _io(nc, "kmax_d", [8, 1], F32)
    wq = _io(nc, "wqT1", [D, NQH * HD], F32)
    wo = _io(nc, "woT1", [D, D], F32)
    qcos = _io(nc, "qcos", [128, TPC], F32)
    qsin = _io(nc, "qsinadj", [128, TPC], F32)
    thresh = _io(nc, "thresh", [128, 64], F32)
    attnT = _io(nc, "attn1T", [D, TPC], F32, out=True)
    scr = {
        "rrow": nc.dram_tensor("scr_rrow", [1, 512], F32).ap(),
        "kmax": nc.dram_tensor("scr_kmax", [8, 1], F32).ap(),
        "bneg": nc.dram_tensor("scr_bneg", [1, NQH], F32).ap(),
    }
    ctx, st = _mk_ctx(nc)
    with st:
        xn_sb = ctx.sbuf("xn_sb", [128, DCH * TPC], BF16)
        emit_load_xn(ctx, xnT, xn_sb)
        s = ctx.sems

        def kv_loader(sync, kvh, kdst, vdst, kvld):
            kvld.dma(sync, kdst[:, :],
                     kfull[kvh * 128:(kvh + 1) * 128, :], kvh)
            vsrc = vfull[:, kvh * 128:(kvh + 1) * 128]
            kvld.dma(sync,
                     vdst[:, :].rearrange("p (kb dv) -> p kb dv", kb=32),
                     vsrc.rearrange("(kb p) dv -> p kb dv", p=128), kvh)

        def knorm_setup(c3):
            def sec_km(g):
                g.dma_start(out=scr["kmax"], in_=kmax_in).then_inc(
                    s.h("at_kmd"), 16)
                s.bump("at_kmd", 16)

            ctx.block.gpsimd(sec_km)

        emit_attn(ctx, 1, xn_sb, wq, None, (qcos, qsin), kv_loader,
                  knorm_setup, thresh, wo, None, attnT, scr)
    return nc



def emit_f1(ctx, xn_sb, x1T, attnT, bwT, bb, rwT, rbrow, emap, AT, Bf, Rmat,
            ident_f, x2T, scr, dbg=None):
    s, block = ctx.sems, ctx.block
    rw_sb = ctx.sbuf("f1_rw", [128, DCH * 16], BF16)
    rbb = ctx.sbuf("f1_rbb", [128, 16], F32)
    idf = ctx.sbuf("f1_idf", [128, 128], F32)
    Rsb = ctx.sbuf("f1_R", [128, 128], BF16)
    Msb = ctx.sbuf("f1_M", [128, 8], BF16)
    iot = ctx.sbuf("f1_iot", [15, 8], F32)
    emapb = ctx.sbuf("f1_em", [15, 1], I32)
    emapf = ctx.sbuf("f1_emf", [15, 1], F32)
    lg = ctx.sbuf("f1_lg", [128, 16], F32)
    pr = ctx.sbuf("f1_pr", [128, 16], F32)
    pr_b = ctx.sbuf("f1_prb", [128, 16], F32)
    rsm2 = ctx.sbuf("f1_rsm2", [128, 1], F32)
    rs_b = ctx.sbuf("f1_rsb", [128, 1], F32)
    rnk = ctx.sbuf("f1_rnk", [128, 16], F32)
    jkb = ctx.sbuf("f1_jkb", [128, 256], F32)
    gt = ctx.sbuf("f1_gt", [128, 240], F32)
    rmx = ctx.sbuf("f1_rmx", [128, 1], F32)
    rsm = ctx.sbuf("f1_rsm", [128, 1], F32)
    selT = ctx.sbuf("f1_selT", [128, 128], BF16)
    selp = ctx.sbuf("f1_selp", [128, 128], F32)
    wt_sb = ctx.sbuf("f1_wt", [128, 128], BF16)
    wexp = ctx.sbuf("f1_wx", [128, TPC], F32)
    h8w = ctx.sbuf("f1_h8w", [128, TPC], BF16)
    Bfsb = ctx.sbuf("f1_Bf", [128, D], BF16)
    bbc = ctx.sbuf("f1_bb", [128, DCH], F32)
    xb_o = [ctx.sbuf(f"f1_xb{i}", [128, 512], F32) for i in range(2)]
    at_b = [ctx.sbuf(f"f1_at{i}", [128, 512], F32) for i in range(2)]
    x1_b = [ctx.sbuf(f"f1_x1{i}", [128, 512], F32) for i in range(2)]
    rsl = DStream(s, "f1_rsl", per=2)

    def sec_cl(g):
        g.dma_start(out=Rsb[:, :], in_=Rmat).then_inc(s.h("f1_cl"), 16)
        g.dma_start(out=Bfsb[:, :], in_=Bf).then_inc(s.h("f1_cl"), 16)
        g.dma_start(out=rw_sb[:, :].rearrange("p (dc n) -> p dc n",
                                              n=16)[:, :, 0:15],
                    in_=rwT.rearrange("(dc p) n -> p dc n", p=128)).then_inc(
            s.h("f1_cl"), 16)
        s.bump("f1_cl", 48)
        g.iota(iot[:, :], [[1, 8]], base=0, channel_multiplier=0,
               allow_small_or_imprecise_dtypes=True).then_inc(s.h("f1_io"), 1)
        s.bump("f1_io", 1)
        g.dma_start(out=rbb[:, 0:15],
                    in_=rbrow.broadcast_to([128, 15])).then_inc(
            s.h("f1_cl2"), 16)
        s.bump("f1_cl2", 16)

    def sec_sl(sync):
        sync.dma_start(out=idf[:, :], in_=ident_f).then_inc(s.h("f1_sl"), 16)
        sync.dma_start(out=emapb[:, :], in_=emap).then_inc(s.h("f1_sl"), 16)
        sync.dma_start(out=bbc[:, :], in_=bb).then_inc(s.h("f1_sl"), 16)
        s.bump("f1_sl", 48)

    block.gpsimd(sec_cl)
    block.sync(sec_sl)

    def sec_mprep(v):
        v.wait_ge(s.h("f1_sl"), 48)
        v.wait_ge(s.h("f1_io"), 1)
        v.tensor_copy(emapf[:, :], emapb[:, :])
        v.memset(Msb[:, :], 0.0).then_inc(s.h("f1_mc"), 1)
        s.bump("f1_mc", 1)
        v.memset(selT[:, :], 0.0)
        v.memset(wt_sb[:, :], 0.0)
        v.memset(selp[:, :], 0.0)
        v.memset(rsm[:, :], 0.0)
        v.wait_ge(s.h("f1_mb"), 32)
        v.tensor_single_scalar(Msb[0:15, :], iot[:, :], emapf[:, :],
                               ALU.is_equal).then_inc(s.h("f1_m"), 1)
        s.bump("f1_m", 1)

    def sec_mgp(g):
        g.wait_ge(s.h("f1_mc"), 1)
        g.dma_start(out=scr["em"], in_=emapf[:, :]).then_inc(s.h("f1_mb"), 16)
        g.wait_ge(s.h("f1_mb"), 16)
        g.dma_start(out=emapf[:, :], in_=scr["em"]).then_inc(s.h("f1_mb"), 16)
        s.bump("f1_mb", 32)

    block.vector(sec_mprep)
    block.gpsimd(sec_mgp)

    with ctx.scoped() as c2:
        plog = [c2.psum(f"f1_pl{i}", [128, 16], F32) for i in range(2)]
        pst = c2.psum("f1_pst", [16, 128], F32)
        pwt = c2.psum("f1_pwt", [8, 128], F32)
        pwx = c2.psum("f1_pwx", [128, 128], F32)

        def sec_rpe(t):
            t.wait_ge(s.h("f1_cl"), 48)
            t.wait_ge(s.h("xnld"), 16 * DCH)
            t.wait_ge(s.h("f1_sl"), 48)
            t.wait_ge(s.h("f1_m"), 1)
            for tb in range(9):
                if tb < 8:
                    if tb >= 2:
                        t.wait_ge(s.h("f1_lgv"), tb - 1)
                    ins = None
                    for dc in range(DCH):
                        ins = t.matmul(plog[tb % 2][:, 0:15],
                                       xn_sb[:, dc * TPC + tb * 128:
                                             dc * TPC + tb * 128 + 128],
                                       rw_sb[:, dc * 16:dc * 16 + 15],
                                       start=(dc == 0), stop=(dc == DCH - 1))
                    ins.then_inc(s.h("f1_plg"), 1)
                if tb >= 1:
                    tp = tb - 1
                    t.wait_ge(s.h("f1_sel"), tp + 1)
                    if tp >= 1:
                        t.wait_ge(s.h("f1_stc"), tp)
                    t.transpose(pst[:, :], selp[:, 0:16], idf[:, :]).then_inc(
                        s.h("f1_st"), 1)
                    t.wait_ge(s.h("f1_stc"), tp + 1)
                    if tp >= 1:
                        t.wait_ge(s.h("f1_wtc"), tp)
                    t.matmul(pwt[:, :], Msb[:, :], selT[:, :], start=True,
                             stop=True).then_inc(s.h("f1_wt"), 1)
                    t.wait_ge(s.h("f1_wtc"), tp + 1)
                    if tp >= 1:
                        t.wait_ge(s.h("f1_wx"), tp)
                    t.matmul(pwx[:, :], Rsb[:, :], wt_sb[:, :], start=True,
                             stop=True).then_inc(s.h("f1_pwx"), 1)
            s.bump("f1_plg", 8)
            s.bump("f1_st", 8)
            s.bump("f1_wt", 8)
            s.bump("f1_pwx", 8)

        def sec_rv(v):
            v.wait_ge(s.h("f1_cl2"), 16)
            for tb in range(9):
                if tb < 8:
                    v.wait_ge(s.h("f1_plg"), tb + 1)
                    v.wait_ge(s.h("f1_ex"), tb)
                    v.tensor_add(lg[:, 0:15], plog[tb % 2][:, 0:15],
                                 rbb[:, 0:15])
                    v.reduce_max(rmx[:, :], lg[:, 0:15], axis=AX.X,
                                 negate=True).then_inc(s.h("f1_lgv"), 1)
                    v.wait_ge(s.h("f1_ex"), tb + 1)
                    v.reciprocal(rsm2[:, :], rsm[:, :])
                    v.memset(rsm[:, :], 0.0).then_inc(s.h("f1_rc"), 1)
                    v.wait_ge(s.h("f1_rb"), 32 * (tb + 1))
                    for j in range(15):
                        v.tensor_tensor(
                            gt[:, j * 15:(j + 1) * 15], pr_b[:, 0:15],
                            pr_b[:, j:j + 1].broadcast_to([128, 15]),
                            ALU.is_gt)
                    v.reduce_sum(
                        rnk[:, 0:15],
                        gt[:, 0:225].rearrange("p (j i) -> p i j", i=15),
                        axis=AX.X)
                    v.memset(jkb[:, :], 0.0)
                    if tb >= 1:
                        v.wait_ge(s.h("f1_st"), tb)
                    v.scalar_tensor_tensor(selp[:, 0:15], rnk[:, 0:15], 6.5,
                                           pr_b[:, 0:15], ALU.is_ge,
                                           ALU.mult)
                    v.memset(jkb[:, :], 0.0)
                    v.tensor_scalar(selp[:, 0:15], selp[:, 0:15], rs_b[:, :],
                                    None, ALU.mult).then_inc(s.h("f1_sel"), 1)
                if tb >= 1:
                    tp = tb - 1
                    v.wait_ge(s.h("f1_pwx"), tp + 1)
                    v.tensor_copy(wexp[:, tp * 128:(tp + 1) * 128],
                                  pwx[:, :]).then_inc(s.h("f1_wx"), 1)
            s.bump("f1_lgv", 8)
            s.bump("f1_rc", 8)
            s.bump("f1_sel", 8)
            s.bump("f1_wx", 8)

        def sec_ra(sc):
            for tb in range(9):
                if tb < 8:
                    sc.wait_ge(s.h("f1_lgv"), tb + 1)
                    if tb >= 1:
                        sc.wait_ge(s.h("f1_rs"), 32 * tb)
                    sc.activation(pr[:, 0:15], lg[:, 0:15], AF.Exp,
                                  bias=rmx[:, :],
                                  accum_out=rsm[:, :]).then_inc(s.h("f1_ex"), 1)
                if tb >= 1:
                    tp = tb - 1
                    sc.wait_ge(s.h("f1_st"), tp + 1)
                    sc.copy(selT[0:16, :], pst[:, :]).then_inc(s.h("f1_stc"), 1)
                    sc.wait_ge(s.h("f1_wt"), tp + 1)
                    sc.copy(wt_sb[0:8, :], pwt[:, :]).then_inc(s.h("f1_wtc"), 1)
            s.bump("f1_ex", 8)
            s.bump("f1_stc", 8)
            s.bump("f1_wtc", 8)

        def sec_rgp(g):
            for tb in range(8):
                g.wait_ge(s.h("f1_rc"), tb + 1)
                g.dma_start(out=scr["pr"], in_=pr[:, 0:15]).then_inc(
                    s.h("f1_rs"), 16)
                g.dma_start(out=scr["rs"], in_=rsm2[:, :]).then_inc(
                    s.h("f1_rs"), 16)
                g.wait_ge(s.h("f1_rs"), 32 * (tb + 1))
                g.dma_start(out=pr_b[:, 0:15], in_=scr["pr"]).then_inc(
                    s.h("f1_rb"), 16)
                g.dma_start(out=rs_b[:, :], in_=scr["rs"]).then_inc(
                    s.h("f1_rb"), 16)
                g.wait_ge(s.h("f1_rb"), 32 * (tb + 1))
                g.wait_ge(s.h("f1_sel"), tb + 1)
            s.bump("f1_rs", 32 * 8)
            s.bump("f1_rb", 32 * 8)

        block.tensor(sec_rpe)
        block.vector(sec_rv)
        block.scalar(sec_ra)
        block.gpsimd(sec_rgp)

    with ctx.scoped() as c3:
        ph8 = [c3.psum(f"f1_ph8{i}", [128, 512], F32) for i in range(2)]
        atball = ctx.sbuf("f1_atb", [128, DCH * 128], BF16)

        def sec_h8l(g):
            for dc in range(DCH):
                g.dma_start(out=atball[:, dc * 128:(dc + 1) * 128],
                            in_=AT[dc * 128:(dc + 1) * 128, :]).then_inc(
                    s.h("f1_atl"), 16)
            s.bump("f1_atl", 16 * DCH)

        def sec_h8pe(t):
            t.wait_ge(s.h("f1_wx"), 8)
            t.wait_ge(s.h("f1_lgv"), 8)
            t.wait_ge(s.h("f1_stc"), 8)
            t.wait_ge(s.h("f1_wtc"), 8)
            t.wait_ge(s.h("f1_atl"), 16 * DCH)
            for tt in range(NTT):
                ins = None
                for dc in range(DCH):
                    ins = t.matmul(ph8[tt][:, :],
                                   atball[:, dc * 128:(dc + 1) * 128],
                                   xn_sb[:, dc * TPC + tt * 512:
                                         dc * TPC + tt * 512 + 512],
                                   start=(dc == 0), stop=(dc == DCH - 1))
                ins.then_inc(s.h("f1_h8"), 1)
            s.bump("f1_h8", NTT)

        def sec_h8v(v):
            for tt in range(NTT):
                v.wait_ge(s.h("f1_h8"), tt + 1)
                v.tensor_tensor(h8w[:, tt * 512:tt * 512 + 512],
                                ph8[tt][:, :],
                                wexp[:, tt * 512:tt * 512 + 512],
                                ALU.mult).then_inc(s.h("f1_h8w"), 1)
            s.bump("f1_h8w", NTT)

        block.gpsimd(sec_h8l)
        block.tensor(sec_h8pe)
        block.vector(sec_h8v)

    def tail_mm(t, dt, tt, psum):
        t.wait_ge(s.h("f1_h8w"), NTT)
        t.wait_ge(s.h("f1_cl"), 48)
        return t.matmul(psum[:, :], Bfsb[:, dt * 128:(dt + 1) * 128],
                        h8w[:, tt * 512:tt * 512 + 512],
                        start=False, stop=True)

    def b_act(eng, dt, tt, psum, idx):
        eng.wait_ge(s.h("f1_sl"), 48)
        if idx >= 2:
            ep_wait(s, "f1b", eng, idx - 1)
        return eng.activation(xb_o[idx % 2][:, :], psum[:, :], AF.Identity,
                              bias=bbc[:, dt:dt + 1])

    def b_vec(eng, dt, tt, psum, idx):
        k = idx
        rsl.wait(eng, k)
        eng.tensor_add(at_b[k % 2][:, :], at_b[k % 2][:, :], x1_b[k % 2][:, :])
        return eng.tensor_add(xb_o[idx % 2][:, :], xb_o[idx % 2][:, :],
                              at_b[k % 2][:, :])

    def _rload(eng, idx):
        dt, tt = idx // NTT, idx % NTT
        rsl.dma(eng, at_b[idx % 2][:, :],
                attnT[dt * 128:(dt + 1) * 128, tt * 512:tt * 512 + 512], idx)
        rsl.dma(eng, x1_b[idx % 2][:, :],
                x1T[dt * 128:(dt + 1) * 128, tt * 512:tt * 512 + 512], idx)

    def sec_rsl0(sync):
        _rload(sync, 0)
        _rload(sync, 1)

    block.sync(sec_rsl0)

    def b_sync(eng, dt, tt, psum, idx):
        if idx + 2 < DCH * NTT:
            _rload(eng, idx + 2)
        return eng.dma_start(out=x2T[dt * 128:(dt + 1) * 128,
                                     tt * 512:tt * 512 + 512],
                             in_=xb_o[idx % 2][:, :])

    if dbg is not None:
        def sec_dbg(sync):
            sync.wait_ge(s.h("f1_wx"), 8)
            sync.dma_start(out=dbg["wexp"], in_=wexp[:, :]).then_inc(
                s.h("f1_dbg"), 16)
            sync.wait_ge(s.h("f1_h8w"), NTT)
            sync.dma_start(out=dbg["h8w"], in_=h8w[:, :]).then_inc(
                s.h("f1_dbg"), 16)
            sync.wait_ge(s.h("f1_wx"), 8)
            sync.dma_start(out=dbg["wt"], in_=wt_sb[:, :]).then_inc(
                s.h("f1_dbg"), 16)
            sync.dma_start(out=dbg["selT"], in_=selT[:, :]).then_inc(
                s.h("f1_dbg"), 16)
            sync.dma_start(out=dbg["pr"], in_=pr[:, :]).then_inc(
                s.h("f1_dbg"), 16)
            sync.dma_start(out=dbg["selp"], in_=selp[:, :]).then_inc(
                s.h("f1_dbg"), 16)
            sync.dma_start(out=dbg["rw"], in_=rw_sb[:, :]).then_inc(
                s.h("f1_dbg"), 16)
            sync.dma_start(out=dbg["lg"], in_=lg[:, :]).then_inc(
                s.h("f1_dbg"), 16)
            sync.dma_start(out=dbg["rmx"], in_=rmx[:, :]).then_inc(
                s.h("f1_dbg"), 16)
            sync.dma_start(out=dbg["rsm"], in_=rsm[:, :]).then_inc(
                s.h("f1_dbg"), 16)
            sync.wait_ge(s.h("f1_dbg"), 160)
            s.bump("f1_dbg", 160)

        block.sync(sec_dbg)

    emit_proj(ctx, "f1b", bwT, True, D, xn_sb,
              {"act": b_act, "vec": b_vec, "sync": b_sync, "final": "sync",
               "dma_stages": {"sync"}}, tail_mm=tail_mm,
              gate=("f1_h8w", NTT))

    def sec_fin(sync):
        ep_wait(s, "f1b", sync, DCH * NTT)

    block.sync(sec_fin)


def build_F1(debug=False):
    nc = _bass()
    xnT = _io(nc, "xn1T", [D, TPC], BF16)
    x1T = _io(nc, "x1T", [D, TPC], F32)
    attnT = _io(nc, "attn1T", [D, TPC], F32)
    bwT = _io(nc, "bwT", [D, D], F32)
    bb = _io(nc, "bb", [128, DCH], F32)
    rwT = _io(nc, "rwT", [D, 15], F32)
    rbrow = _io(nc, "rbrow", [1, 15], F32)
    emap = _io(nc, "emap", [15, 1], I32)
    AT = _io(nc, "AT", [D, 128], F32)
    Bf = _io(nc, "Bf", [128, D], F32)
    Rmat = _io(nc, "Rmat", [128, 128], F32)
    ident_f = _io(nc, "ident_f", [128, 128], F32)
    x2T = _io(nc, "x2T", [D, TPC], F32, out=True)
    dbg = None
    if debug:
        dbg = {"wexp": _io(nc, "d_wexp", [128, TPC], F32, out=True),
               "h8w": _io(nc, "d_h8w", [128, TPC], BF16, out=True),
               "wt": _io(nc, "d_wt", [128, 128], BF16, out=True),
               "selT": _io(nc, "d_selT", [128, 128], BF16, out=True),
               "pr": _io(nc, "d_pr", [128, 16], F32, out=True),
               "selp": _io(nc, "d_selp", [128, 128], F32, out=True),
               "rw": _io(nc, "d_rw", [128, DCH * 16], BF16, out=True),
               "lg": _io(nc, "d_lg", [128, 16], F32, out=True),
               "rmx": _io(nc, "d_rmx", [128, 1], F32, out=True),
               "rsm": _io(nc, "d_rsm", [128, 1], F32, out=True)}
    ctx, st = _mk_ctx(nc)
    with st:
        xn_sb = ctx.sbuf("xn_sb", [128, DCH * TPC], BF16)
        emit_load_xn(ctx, xnT, xn_sb)
        scr = {"pr": nc.dram_tensor("f1s_pr", [128, 15], F32).ap(),
               "rs": nc.dram_tensor("f1s_rs", [128, 1], F32).ap(),
               "em": nc.dram_tensor("f1s_em", [15, 1], F32).ap()}
        emit_f1(ctx, xn_sb, x1T, attnT, bwT, bb, rwT, rbrow, emap, AT, Bf,
                Rmat, ident_f, x2T, scr, dbg=dbg)
    return nc



def _prep_core_dicts(inputs):
    x = np.asarray(inputs["L0_attn_x"])
    ident = np.eye(128)
    shared = {
        "ident_f": ident.astype(np.float32),
        "ident_b": ident.astype(bfloat16),
        "wq": np.asarray(inputs["L0_attn_w_q"]),
        "bq": np.ascontiguousarray(
            np.asarray(inputs["L0_attn_b_q"]).reshape(32, 128).T),
        "wk": np.asarray(inputs["L0_attn_w_k"]),
        "bk": np.ascontiguousarray(
            np.asarray(inputs["L0_attn_b_k"]).reshape(8, 128).T),
        "wv": np.asarray(inputs["L0_attn_w_v"]),
        "bvrow": np.asarray(inputs["L0_attn_b_v"]).reshape(1, 1024),
        "wo": np.asarray(inputs["L0_attn_w_o"]),
        "bo": np.ascontiguousarray(
            np.asarray(inputs["L0_attn_b_o"]).reshape(32, 128).T),
        "wqT1": np.ascontiguousarray(np.asarray(inputs["L1_attn_Wq"]).T),
        "wkrT": np.ascontiguousarray(np.asarray(inputs["L1_attn_Wkr"]).T),
        "wdkT": np.ascontiguousarray(np.asarray(inputs["L1_attn_Wdk"]).T),
        "wupkT": np.ascontiguousarray(np.asarray(inputs["L1_attn_Wupk"]).T),
        "wupvT": np.ascontiguousarray(np.asarray(inputs["L1_attn_Wupv"]).T),
        "woT1": np.ascontiguousarray(np.asarray(inputs["L1_attn_Wo"]).T),
        "bwT": np.ascontiguousarray(np.asarray(inputs["L1_ffn_base_W"]).T),
        "bb": np.ascontiguousarray(
            np.asarray(inputs["L1_ffn_base_b"]).reshape(32, 128).T),
        "rwT": np.ascontiguousarray(np.asarray(inputs["L1_ffn_router_W"]).T),
        "rbrow": np.asarray(inputs["L1_ffn_router_b"]).reshape(1, 15),
        "emap": np.asarray(inputs["L1_ffn_expert_map"]).reshape(15, 1),
        "AT": np.ascontiguousarray(
            np.asarray(inputs["L1_ffn_A"]).reshape(128, D).T),
        "Bf": np.ascontiguousarray(
            np.asarray(inputs["L1_ffn_B"]).transpose(0, 2, 1).reshape(128, D)),
    }
    R = np.zeros((128, 128), np.float32)
    for e in range(8):
        R[e, e * 16:(e + 1) * 16] = 2.0
    shared["Rmat"] = R
    w1r_all = np.ascontiguousarray(
        np.asarray(inputs["L0_ffn_w1"]).reshape(H, 32, 64).transpose(1, 2, 0))
    s1r_all = np.ascontiguousarray(
        np.asarray(inputs["L0_ffn_s1"]).reshape(H, 32, 2).transpose(1, 2, 0))
    w3r_all = np.ascontiguousarray(
        np.asarray(inputs["L0_ffn_w3"]).reshape(H, 32, 64).transpose(1, 2, 0))
    s3r_all = np.ascontiguousarray(
        np.asarray(inputs["L0_ffn_s3"]).reshape(H, 32, 2).transpose(1, 2, 0))
    w2r_all = np.zeros((HPKB, 64, D), np.int32)
    w2r_all[:HKB] = np.asarray(inputs["L0_ffn_w2"]).reshape(
        D, HKB, 64).transpose(1, 2, 0)
    s2r_all = np.zeros((HPKB, 2, D), np.float32)
    s2r_all[:HKB] = np.asarray(inputs["L0_ffn_s2"]).reshape(
        D, HKB, 2).transpose(1, 2, 0)

    cd = []
    for c in range(NCORES):
        bc, t0 = c // 4, (c % 4) * TPC
        d = dict(shared)
        d["x0T"] = np.ascontiguousarray(x[bc, t0:t0 + TPC, :].T)
        d["w1r"] = w1r_all[4 * c:4 * c + 4]
        d["s1r"] = s1r_all[4 * c:4 * c + 4]
        d["w3r"] = w3r_all[4 * c:4 * c + 4]
        d["s3r"] = s3r_all[4 * c:4 * c + 4]
        d["w2r"] = w2r_all[11 * c:11 * c + 11]
        d["s2r"] = s2r_all[11 * c:11 * c + 11]
        pos = np.arange(t0, t0 + TPC, dtype=np.float64)
        inv = 1.0 / (10000.0 ** (np.arange(0, 128, 2) / 128.0))
        fr = np.outer(inv, pos)
        d["qcos"] = np.concatenate([np.cos(fr), np.cos(fr)]).astype(np.float32)
        d["qsinadj"] = np.concatenate([np.sin(fr), -np.sin(fr)]).astype(
            np.float32)
        fr32 = fr[:32]
        kc1 = np.concatenate([np.cos(fr32), np.cos(fr32)])
        ks1 = np.concatenate([np.sin(fr32), -np.sin(fr32)])
        d["kcos"] = np.concatenate([kc1, kc1]).astype(np.float32)
        d["ksinadj"] = np.concatenate([ks1, ks1]).astype(np.float32)
        thr = np.zeros((2, 32), np.float64)
        for qt in range(2):
            for ch in range(32):
                thr[qt, ch] = 128 * ch - (t0 + 512 * qt)
        d["thresh"] = np.broadcast_to(
            thr.reshape(1, 64), (128, 64)).astype(np.float32).copy()
        cd.append(d)
    return cd


def _run(nc, in_maps, outs):
    res = run_bass_kernel_spmd(nc, in_maps, core_ids=list(range(NCORES)))
    return [{k: np.asarray(r[k]) for k in outs} for r in res.results]


def _sel(cd, keys, extra=None):
    maps = []
    for c in range(NCORES):
        m = {k: cd[c][k] for k in keys}
        if extra is not None:
            m.update(extra[c])
        maps.append(m)
    return maps


def kernel(**inputs):
    cd = _prep_core_dicts(inputs)

    r = _run(build_LN("x0T", "xn0T"), _sel(cd, ["x0T"]), ["xn0T"])
    xn0T = [r[c]["xn0T"] for c in range(NCORES)]

    r = _run(build_DQ(),
             _sel(cd, ["w1r", "s1r", "w3r", "s3r", "w2r", "s2r"]),
             ["dq1", "dq3", "dq2"])
    w1t = np.concatenate([r[c]["dq1"] for c in range(NCORES)], axis=0)
    w3t = np.concatenate([r[c]["dq3"] for c in range(NCORES)], axis=0)
    w2t = np.concatenate([r[c]["dq2"] for c in range(NCORES)], axis=0)

    r = _run(build_A0KV(),
             _sel(cd, ["wk", "bk", "wv", "bvrow"],
                  [{"xn0T": xn0T[c]} for c in range(NCORES)]),
             ["agkv_in"])
    agkv = [np.concatenate([r[g * 4 + i]["agkv_in"] for i in range(4)], axis=0)
            for g in range(2)]

    r = _run(build_A0(),
             _sel(cd, ["wq", "bq", "wo", "bo"],
                  [{"xn0T": xn0T[c], "agkv": agkv[c // 4]}
                   for c in range(NCORES)]),
             ["attn0T"])
    attn0T = [r[c]["attn0T"] for c in range(NCORES)]

    r = _run(build_F0(),
             _sel(cd, ["x0T"],
                  [{"xn0T": xn0T[c], "w1t": w1t, "w3t": w3t, "w2t": w2t,
                    "attn0T": attn0T[c]} for c in range(NCORES)]),
             ["x1T"])
    x1T = [r[c]["x1T"] for c in range(NCORES)]

    r = _run(build_LN("x1T", "xn1T"),
             [{"x1T": x1T[c]} for c in range(NCORES)], ["xn1T"])
    xn1T = [r[c]["xn1T"] for c in range(NCORES)]

    r = _run(build_A1KV(),
             _sel(cd, ["wkrT", "wdkT", "kcos", "ksinadj"],
                  [{"xn1T": xn1T[c]} for c in range(NCORES)]),
             ["agkc_in"])
    agkc = [np.concatenate([r[g * 4 + i]["agkc_in"] for i in range(4)], axis=0)
            for g in range(2)]

    r = _run(build_A1PRE(),
             _sel(cd, ["wupkT", "wupvT"],
                  [{"agkc": agkc[c // 4]} for c in range(NCORES)]),
             ["kfull", "vfull", "kmax_d"])
    kfull = [r[c]["kfull"] for c in range(NCORES)]
    vfull = [r[c]["vfull"] for c in range(NCORES)]
    kmax_d = [r[c]["kmax_d"] for c in range(NCORES)]

    r = _run(build_A1(),
             _sel(cd, ["wqT1", "woT1", "qcos", "qsinadj", "thresh"],
                  [{"xn1T": xn1T[c], "kfull": kfull[c], "vfull": vfull[c],
                    "kmax_d": kmax_d[c]} for c in range(NCORES)]),
             ["attn1T"])
    attn1T = [r[c]["attn1T"] for c in range(NCORES)]

    r = _run(build_F1(),
             _sel(cd, ["bwT", "bb", "rwT", "rbrow", "emap", "AT", "Bf",
                       "Rmat", "ident_f"],
                  [{"xn1T": xn1T[c], "x1T": x1T[c], "attn1T": attn1T[c]}
                   for c in range(NCORES)]),
             ["x2T"])

    out = np.empty((B, S, D), np.float32)
    for c in range(NCORES):
        bc, t0 = c // 4, (c % 4) * TPC
        out[bc, t0:t0 + TPC, :] = r[c]["x2T"].T
    return out
